# version 15
# baseline (speedup 1.0000x reference)
import numpy as np
import ml_dtypes
import concourse.bass as bass
import concourse.mybir as mybir
from concourse.bass_utils import run_bass_kernel_spmd

F32 = mybir.dt.float32
BF16 = mybir.dt.bfloat16
I32 = mybir.dt.int32
AF = mybir.ActivationFunctionType
ALU = mybir.AluOpType
AX = mybir.AxisListType

ENGINES = ("sync", "scalar", "vector", "tensor", "gpsimd")
NEG_BIG = -1.0e9


def make_cfg(S=2048, D=4096, I=11008, NH=32, NKV=8, G=512, phases=("A1", "A2", "B"), dbg=False):
    cfg = dict(S=S, D=D, I=I, NH=NH, NKV=NKV, G=G, phases=phases, dbg=dbg)
    cfg["C"] = D // 128
    cfg["NQT"] = S // 128
    cfg["NG"] = S // G
    cfg["QTG"] = G // 128
    cfg["ICN"] = I // 128
    cfg["DBH"] = D // 256
    cfg["WOB"] = D // 512
    cfg["NREP"] = NH // NKV
    cfg["SCALE"] = 128 ** -0.5
    assert S % G == 0 and G % 128 == 0 and D % 512 == 0 and I % 128 == 0
    return cfg


class Sem:
    def __init__(self, nc, name):
        self.h = nc.alloc_semaphore(name)
        self.name = name
        self.n = 0


class KB:
    def __init__(self, cfg):
        self.cfg = cfg
        nc = bass.Bass("TRN2", target_bir_lowering=False)
        self.nc = nc
        self.prog = {e: [] for e in ENGINES}
        self.lastw = {}
        self.sem = {k: Sem(nc, k) for k in ("pe", "ve", "ac", "gp", "ph")}
        self.dsems = {}
        self.qfams = {"gpsimd": set(), "sync": set(), "scalar": set()}

    def w(self, e, rel):
        if rel is None:
            return
        sem, val = rel
        if val is None or val <= 0:
            return
        key = (e, sem.name)
        if self.lastw.get(key, -1) >= val:
            return
        self.lastw[key] = val
        eng = getattr(self.nc, e)
        h = sem.h
        self.prog[e].append(lambda eng=eng, h=h, val=val: eng.wait_ge(h, val))

    def dma(self, q, fam, out, in_, **kw):
        if fam not in self.dsems:
            self.dsems[fam] = Sem(self.nc, "d_" + fam)
        self.qfams[q].add(fam)
        sem = self.dsems[fam]
        sem.n += 16
        eng = getattr(self.nc, q)
        h = sem.h
        self.prog[q].append(
            lambda eng=eng, h=h, out=out, in_=in_, kw=kw: eng.dma_start(out=out, in_=in_, **kw).then_inc(h, 16))
        return (sem, sem.n)

    def _emit(self, e, make, inc):
        eng = getattr(self.nc, e)
        if inc:
            sem = {"tensor": self.sem["pe"], "vector": self.sem["ve"],
                   "scalar": self.sem["ac"], "gpsimd": self.sem["gp"]}[e]
            sem.n += 1
            h = sem.h
            self.prog[e].append(lambda eng=eng, make=make, h=h: make(eng).then_inc(h, 1))
            return (sem, sem.n)
        self.prog[e].append(lambda eng=eng, make=make: make(eng))
        return None

    def mm(self, out, lhsT, rhs, start, stop, inc=False):
        return self._emit("tensor",
                          lambda eng, out=out, lhsT=lhsT, rhs=rhs, start=start, stop=stop:
                          eng.matmul(out, lhsT, rhs, start=start, stop=stop), inc)

    def tr(self, out, in_, inc=False):
        ident = self.cst_id
        return self._emit("tensor",
                          lambda eng, out=out, in_=in_, ident=ident: eng.transpose(out, in_, ident[:]), inc)

    def act(self, out, in_, func, bias=0.0, scale=1.0, accum_out=None, inc=True):
        return self._emit("scalar",
                          lambda eng, out=out, in_=in_, func=func, bias=bias, scale=scale, acc=accum_out:
                          eng.activation(out, in_, func, bias=bias, scale=scale, accum_out=acc), inc)

    def ts(self, e, out, in0, s1, s2, op0, op1, accum_out=None, inc=True):
        return self._emit(e,
                          lambda eng, out=out, in0=in0, s1=s1, s2=s2, op0=op0, op1=op1, acc=accum_out:
                          eng.tensor_scalar(out, in0, s1, s2, op0, op1, accum_out=acc), inc)

    def tt(self, e, out, in0, in1, op, inc=True):
        return self._emit(e,
                          lambda eng, out=out, in0=in0, in1=in1, op=op: eng.tensor_tensor(out, in0, in1, op), inc)

    def red(self, out, in_, op, negate=False, inc=True):
        return self._emit("vector",
                          lambda eng, out=out, in_=in_, op=op, neg=negate:
                          eng.tensor_reduce(out, in_, AX.X, op, negate=neg), inc)

    def recip(self, out, in_, inc=True):
        return self._emit("vector", lambda eng, out=out, in_=in_: eng.reciprocal(out, in_), inc)

    def copy(self, e, out, in_, inc=True):
        return self._emit(e, lambda eng, out=out, in_=in_: eng.tensor_copy(out, in_), inc)

    def bnstats(self, out, in_, inc=False):
        return self._emit("vector", lambda eng, out=out, in_=in_: eng.bn_stats(out, in_), inc)

    def bnaggr(self, out, in_, inc=True):
        return self._emit("vector", lambda eng, out=out, in_=in_: eng.bn_aggr(out, in_), inc)

    def barrier(self):
        ph = self.sem["ph"]
        for e in ENGINES:
            if e in self.qfams:
                for fam in sorted(self.qfams[e]):
                    self.w(e, (self.dsems[fam], self.dsems[fam].n))
            ph.n += 1
            eng = getattr(self.nc, e)
            h = ph.h
            self.prog[e].append(lambda eng=eng, h=h: eng.sem_inc(h, 1))
        for e in ENGINES:
            self.w(e, (ph, ph.n))

    def declare(self):
        nc, cfg = self.nc, self.cfg
        S, D, I, NH, NKV, G = (cfg[k] for k in ("S", "D", "I", "NH", "NKV", "G"))
        C, NQT, ICN, DBH, WOB = (cfg[k] for k in ("C", "NQT", "ICN", "DBH", "WOB"))
        scratch_kind = "ExternalOutput" if cfg["dbg"] else "Internal"

        self.x = nc.dram_tensor("x", [S, D], F32, kind="ExternalInput").ap()
        self.qkvblob = nc.dram_tensor("qkvblob", [NH + 2 * NKV, 128, C, 128], F32, kind="ExternalInput").ap()
        self.woblob = nc.dram_tensor("woblob", [D // 256, 128, C, 256], F32, kind="ExternalInput").ap()
        self.gwblob = nc.dram_tensor("gwblob", [ICN, 128, C, 128], I32, kind="ExternalInput").ap()
        self.uwblob = nc.dram_tensor("uwblob", [ICN, 128, C, 128], I32, kind="ExternalInput").ap()
        self.dwblob = nc.dram_tensor("dwblob", [DBH, 128, ICN, 256], I32, kind="ExternalInput").ap()
        self.scal3 = nc.dram_tensor("scal3", [4], F32, kind="ExternalInput").ap()
        self.cst = nc.dram_tensor("cst", [2, 128, 128], F32, kind="ExternalInput").ap()
        self.out = nc.dram_tensor("out", [S, D], F32, kind="ExternalOutput").ap()

        self.qT_scr = nc.dram_tensor("qT_scr", [NH, 128, S], BF16, kind=scratch_kind).ap()
        self.oT_scr = nc.dram_tensor("oT_scr", [2, G // 128, 128, NH * 128], BF16, kind="Internal").ap()
        self.kT_scr = nc.dram_tensor("kT_scr", [NKV, 128, S], BF16, kind=scratch_kind).ap()
        self.v_scr = nc.dram_tensor("v_scr", [NQT, 128, NKV * 128], BF16, kind=scratch_kind).ap()
        self.xnT_scr = nc.dram_tensor("xnT_scr", [C, 128, S], BF16, kind=scratch_kind).ap()
        self.r_scr = nc.dram_tensor("r_scr", [S, D], F32, kind=scratch_kind).ap()
        if cfg["dbg"]:
            self.dbg1 = nc.dram_tensor("dbg1", [128, 64], F32, kind="ExternalOutput").ap()
            self.dbg2 = nc.dram_tensor("dbg2", [128, D], F32, kind="ExternalOutput").ap()

        self.psA = [nc.alloc_psum_tensor(f"psA{i}", [128, 512], F32) for i in range(2)]
        self.psB = [nc.alloc_psum_tensor(f"psB{i}", [128, 512], F32) for i in range(2)]
        self.psC = [nc.alloc_psum_tensor(f"psC{i}", [128, 512], F32) for i in range(1)]
        self.psT = [nc.alloc_psum_tensor(f"psT{i}", [128, 1024], BF16) for i in range(3)]

        BASE, TOP = 16512, 229344
        cur = [BASE]

        def place(name, free_elems, dtype, at=None):
            esz = 2 if dtype == BF16 else 4
            off = cur[0] if at is None else at
            t = nc.alloc_sbuf_tensor_at(name, [128, free_elems], dtype, offset=off)
            if at is None:
                cur[0] = off + ((free_elems * esz + 31) // 32) * 32
            assert (off + free_elems * esz) <= TOP, f"SBUF overflow {name}"
            return t

        self.cst_id = place("cst_id", 128, BF16)
        self.cst_lt = place("cst_lt", 128, BF16)
        self.scals = place("scals", 16, F32)
        glob_end = cur[0]

        cur[0] = glob_end
        self.xnT = place("xnT", C * S, BF16)
        self.wbuf = [place(f"wbuf{i}", C * 128, BF16) for i in range(2)]
        self.xtile = [place(f"xtile{i}", D, F32) for i in range(2)]
        self.xn_tm = [place(f"xn_tm{i}", D, BF16) for i in range(2)]
        self.stg = [place(f"stg{i}", 512, BF16) for i in range(4)]
        self.vstg = [place(f"vstg{i}", 512, BF16) for i in range(2)]
        self.vtstg = [place(f"vtstg{i}", 512, BF16) for i in range(2)]
        self.bns = [place(f"bns{i}", 6 * (D // 512), F32) for i in range(2)]
        self.lnv = [place(f"lnv{i}", 8, F32) for i in range(2)]
        assert cur[0] <= TOP, f"A1 overflow {cur[0]}"

        cur[0] = glob_end
        self.Qg = place("Qg", NH * G, BF16)
        self.Kc = place("Kc", NKV * S, BF16)
        self.Vc = place("Vc", NQT * NKV * 128, BF16)
        self.avec = [place(f"avec{i}", 16, F32) for i in range(3)]
        self.xA = [place(f"xA{i}", 256, F32) for i in range(2)]
        self.rA = [place(f"rA{i}", 256, F32) for i in range(2)]
        self.Ss = [place(f"Ss{i}", S, F32) for i in range(2)]
        self.Pn = [place(f"Pn{i}", S, BF16) for i in range(4)]
        self.PT = place("PT", NQT * G, BF16)
        self.ogt = [place(f"ogt{i}", NH * 128, BF16) for i in range(2)]
        self.wo_buf = [place(f"wo_buf{i}", C * 256, BF16) for i in range(2)]
        self.otst = [place(f"otst{i}", G, BF16) for i in range(2)]
        assert cur[0] <= TOP, f"A2 overflow {cur[0]}"

        cur[0] = glob_end
        self.Hbuf = place("Hbuf", ICN * G, BF16)
        self.gtmp = [place(f"gtmp{i}", G, BF16) for i in range(2)]
        self.utmp = [place(f"utmp{i}", G, BF16) for i in range(2)]
        self.Ftmp = [place(f"Ftmp{i}", 256, F32) for i in range(2)]
        self.rB = [place(f"rB{i}", 256, F32) for i in range(2)]
        self.oB = [place(f"oB{i}", 256, F32) for i in range(2)]
        dw_ov = cur[0]
        self.xng = place("xng", C * G, BF16)
        self.gw_buf = [place(f"gw_buf{i}", C * 128, BF16) for i in range(2)]
        self.uw_buf = [place(f"uw_buf{i}", C * 128, BF16) for i in range(2)]
        b_end = cur[0]
        cur[0] = dw_ov
        self.dw_buf = [place(f"dw_buf{i}", ICN * 256, BF16) for i in range(2)]
        assert max(b_end, cur[0]) <= TOP, f"B overflow {max(b_end, cur[0])}"

    def emit_setup(self):
        k = self
        k.dma("gpsimd", "setup", k.cst_id[:], k.cst[0])
        k.dma("gpsimd", "setup", k.cst_lt[:], k.cst[1])
        for i in range(4):
            last = k.dma("gpsimd", "setup", k.scals[:, i:i + 1], k.scal3[None, i:i + 1].broadcast_to([128, 1]))
        self.setup_ld = last

    def emit_a1(self):
        k = self
        cfg = self.cfg
        S, D, NH, NKV, C, NQT = (cfg[x] for x in ("S", "D", "NH", "NKV", "C", "NQT"))
        xnT_v = k.xnT.ap().rearrange("p (c t) -> p c t", c=C)

        xt_rel = [None, None]
        xn_rel = [None, None]
        trp_rel = [None, None]
        lnv_rel = [None, None]
        nb = D // 512
        tr_ctr = 0
        for t in range(NQT):
            sl = t % 2
            k.w("sync", xt_rel[sl])
            x_ld = k.dma("sync", f"xt{sl}", k.xtile[sl][:], k.x[t * 128:(t + 1) * 128, :])
            k.w("vector", x_ld)
            v_bns = None
            for b in range(nb):
                v_bns = k.bnstats(k.bns[sl][:, 6 * b:6 * (b + 1)], k.xtile[sl][:, 512 * b:512 * (b + 1)],
                                  inc=(b == nb - 1))
            k.w("vector", v_bns)
            k.w("vector", lnv_rel[sl])
            k.bnaggr(k.lnv[sl][:, 0:2], k.bns[sl][:])
            v_aggr = (k.sem["ve"], k.sem["ve"].n)
            k.w("scalar", v_aggr)
            a_std = k.act(k.lnv[sl][:, 2:3], k.lnv[sl][:, 1:2], AF.Sqrt, bias=k.scals[:, 3:4], scale=1.0)
            lnv_rel[sl] = a_std
            k.w("vector", a_std)
            v_rc = k.recip(k.lnv[sl][:, 3:4], k.lnv[sl][:, 2:3])
            k.w("vector", v_rc)
            v_nm = k.ts("vector", k.lnv[sl][:, 4:5], k.lnv[sl][:, 0:1], k.lnv[sl][:, 3:4], -1.0,
                        ALU.mult, ALU.mult)
            k.w("vector", v_nm)
            k.w("vector", xn_rel[sl])
            v_norm = k.ts("vector", k.xn_tm[sl][:], k.xtile[sl][:], k.lnv[sl][:, 3:4], k.lnv[sl][:, 4:5],
                          ALU.mult, ALU.add)
            xt_rel[sl] = v_norm
            k.w("tensor", v_norm)
            for cp in range(0, C, 4):
                npk = min(4, C - cp)
                tsl = tr_ctr % 2
                tr_ctr += 1
                k.w("tensor", trp_rel[tsl])
                pe_c = None
                for j in range(npk):
                    pe_c = k.tr(k.psT[tsl][:, 128 * j:128 * (j + 1)],
                                k.xn_tm[sl][:, 128 * (cp + j):128 * (cp + j + 1)],
                                inc=(j == npk - 1))
                xn_rel[sl] = pe_c
                k.w("vector", pe_c)
                trp_rel[tsl] = k.copy("vector", xnT_v[:, cp:cp + npk, t * 128:(t + 1) * 128],
                                      k.psT[tsl][:].rearrange("p (c t) -> p c t", c=8)[:, 0:npk, :])
            if t == 0 and self.cfg["dbg"]:
                k.w("sync", (k.sem["ve"], k.sem["ve"].n))
                k.dma("sync", "dbgf", k.dbg1[:, 0:16], k.scals[:])
                k.dma("sync", "dbgf", k.dbg1[:, 16:24], k.lnv[0][:])
                k.dma("sync", "dbgf", k.dbg1[:, 24:24 + 6 * (D // 512)], k.bns[0][:])
                k.w("gpsimd", (k.sem["ve"], k.sem["ve"].n))
                k.dma("gpsimd", "dbgf2", k.dbg2[:, 0:D], k.xn_tm[0][:])
        ln_done = (k.sem["ve"], k.sem["ve"].n)
        k.w("sync", ln_done)
        for c in range(C):
            k.dma("sync", "xnw", k.xnT_scr[c], xnT_v[:, c, :])

        k.w("tensor", ln_done)
        NOC = NH + 2 * NKV
        w_rel = [None, None]
        ps_rel = [None, None]
        st_rel = [None] * 4
        vs_rel = [None, None]
        vt_rel = [None] * 2
        ntc = S // 512
        step = 0
        for oc in range(NOC):
            wsl = oc % 2
            k.w("gpsimd", w_rel[wsl])
            w_ld = k.dma("gpsimd", f"wb{wsl}", k.wbuf[wsl][:], k.qkvblob[oc].rearrange("p c i -> p (c i)"),
                         max_dma_last_dim=8192)
            k.w("tensor", w_ld)
            is_v = oc >= NH + NKV
            for tc in range(ntc):
                psl = step % 2
                step += 1
                k.w("tensor", ps_rel[psl])
                pe_c = None
                for c in range(C):
                    pe_c = k.mm(k.psB[psl][:], k.wbuf[wsl][:, c * 128:(c + 1) * 128],
                                xnT_v[:, c, tc * 512:(tc + 1) * 512],
                                start=(c == 0), stop=(c == C - 1), inc=(c == C - 1))
                w_rel[wsl] = pe_c
                k.w("scalar", pe_c)
                if not is_v:
                    ssl = step % 4
                    k.w("scalar", st_rel[ssl])
                    a_c = k.act(k.stg[ssl][:], k.psB[psl][:], AF.Copy)
                    ps_rel[psl] = a_c
                    k.w("sync", a_c)
                    if oc < NH:
                        dst = k.qT_scr[oc][:, tc * 512:(tc + 1) * 512]
                    else:
                        dst = k.kT_scr[oc - NH][:, tc * 512:(tc + 1) * 512]
                    st_rel[ssl] = k.dma("sync", f"stg{ssl}", dst, k.stg[ssl][:])
                else:
                    vsl = step % 2
                    k.w("scalar", vs_rel[vsl])
                    a_c = k.act(k.vstg[vsl][:], k.psB[psl][:], AF.Copy)
                    ps_rel[psl] = a_c
                    k.w("tensor", a_c)
                    tsl = step % 2
                    k.w("tensor", trp_rel[tsl])
                    pe_t = None
                    for j in range(4):
                        pe_t = k.tr(k.psT[tsl][:, 128 * j:128 * (j + 1)],
                                    k.vstg[vsl][:, 128 * j:128 * (j + 1)], inc=(j == 3))
                    vs_rel[vsl] = pe_t
                    k.w("scalar", pe_t)
                    k.w("scalar", vt_rel[tsl])
                    a_t = k.act(k.vtstg[tsl][:], k.psT[tsl][:, 0:512], AF.Copy)
                    trp_rel[tsl] = a_t
                    k.w("sync", a_t)
                    kv = oc - NH - NKV
                    last = None
                    for j in range(4):
                        tt_ = tc * 4 + j
                        last = k.dma("sync", f"vt{tsl}", k.v_scr[tt_][:, kv * 128:(kv + 1) * 128],
                                     k.vtstg[tsl][:, 128 * j:128 * (j + 1)])
                    vt_rel[tsl] = last

    def emit_a2(self):
        k = self
        cfg = self.cfg
        S, NH, NKV, G = (cfg[x] for x in ("S", "NH", "NKV", "G"))
        C, NQT, QTG, NREP = (cfg[x] for x in ("C", "NQT", "QTG", "NREP"))
        D = cfg["D"]
        NDB = D // 256
        SCALE = cfg["SCALE"]
        PT_v = k.PT.ap().rearrange("p (j q) -> p j q", j=NQT)
        NSL = 2
        NPN = 4
        LAG_T = 3
        LAG_PV = 5

        ckps = [k.psA[0], k.psA[1]]
        psF = k.psC[0]
        ssp_rel = [None, None]
        Ss_rel = [None] * NSL
        Pn_rel = [None] * NPN
        av_rel = [None] * NSL
        ptp_rel = [None, None, None]
        ot_rel = [None, None]
        otst_rel = [None, None]
        qg_rel = None
        wo_rel = [None, None]
        wo_lds = [None, None]
        og_lds = [None, None]
        ogt_rel = [None, None]
        psF_rel = None
        xA_rel = [None, None]
        rA_rel = [None, None]
        units_done_pe = None
        ck_ctr = 0
        pt_ctr = 0
        un_ctr = 0

        def make_load(gm1, u):
            def go():
                db = u // QTG
                tc = u % QTG
                osl = u % 2
                wsl = db % 2
                if tc == 0:
                    k.w("gpsimd", wo_rel[wsl])
                    wo_lds[wsl] = k.dma("gpsimd", "wo", k.wo_buf[wsl][:],
                                        k.woblob[db].rearrange("p c j -> p (c j)"),
                                        max_dma_last_dim=8192)
                k.w("sync", ogt_rel[osl])
                for fam in ("ot0", "ot1"):
                    if fam in k.dsems:
                        k.w("sync", (k.dsems[fam], k.dsems[fam].n))
                og_lds[osl] = k.dma("sync", f"og{osl}", k.ogt[osl][:], k.oT_scr[gm1 % 2][tc])
            return go

        def make_unit(gm1, u):
            def go():
                nonlocal psF_rel, un_ctr, units_done_pe
                db = u // QTG
                tc = u % QTG
                osl = u % 2
                wsl = db % 2
                un_ctr += 1
                k.w("tensor", wo_lds[wsl])
                k.w("tensor", og_lds[osl])
                k.w("tensor", psF_rel)
                pe_c = None
                for c in range(C):
                    pe_c = k.mm(psF[:, 0:256], k.ogt[osl][:, c * 128:(c + 1) * 128],
                                k.wo_buf[wsl][:, c * 256:(c + 1) * 256],
                                start=(c == 0), stop=(c == C - 1), inc=(c == C - 1))
                wo_rel[wsl] = pe_c
                ogt_rel[osl] = pe_c
                units_done_pe = pe_c
                xsl = un_ctr % 2
                k.w("sync", xA_rel[xsl])
                t0 = (gm1 * QTG + tc) * 128
                o0 = db * 256
                x_ld = k.dma("sync", f"xa{xsl}", k.xA[xsl][:], k.x[t0:t0 + 128, o0:o0 + 256])
                k.w("vector", x_ld)
                k.w("vector", pe_c)
                k.w("vector", rA_rel[xsl])
                ve_r = k.tt("vector", k.rA[xsl][:], psF[:, 0:256], k.xA[xsl][:], ALU.add)
                psF_rel = ve_r
                xA_rel[xsl] = ve_r
                k.w("sync", ve_r)
                rA_rel[xsl] = k.dma("sync", f"ra{xsl}", k.r_scr[t0:t0 + 128, o0:o0 + 256], k.rA[xsl][:])
            return go

        for g in range(cfg["NG"]):
            k.w("scalar", qg_rel)
            q_ld = None
            kv_ld = None
            for h in range(NH):
                q_ld = k.dma("scalar", "qg", k.Qg[:, h * G:(h + 1) * G], k.qT_scr[h][:, g * G:(g + 1) * G])
            for kv in range(NKV):
                kv_ld = k.dma("scalar", "kv", k.Kc[:, kv * S + g * G: kv * S + (g + 1) * G],
                              k.kT_scr[kv][:, g * G:(g + 1) * G])
            for j in range(QTG):
                tt_ = g * QTG + j
                kv_ld = k.dma("scalar", "kv", k.Vc[:, tt_ * NKV * 128:(tt_ + 1) * NKV * 128], k.v_scr[tt_])
            k.w("tensor", q_ld)
            k.w("tensor", kv_ld)
            k.w("scalar", units_done_pe)

            deferred = []
            NU = NDB * QTG
            if g > 0:
                f_loads = [make_load(g - 1, u) for u in range(NU)]
                f_comps = [make_unit(g - 1, u) for u in range(NU)]
                f_loads.pop(0)(); f_loads.pop(0)()
            else:
                f_loads, f_comps = [], []

            def emit_chain(h, qt, sl):
                i = g * QTG + qt
                L = (i + 1) * 128
                nck = (L + 511) // 512
                k.w("vector", (k.sem["ve"], k.sem["ve"].n))
                k.red(k.avec[sl][:, 8:9], k.avec[sl][:, 0:nck], ALU.max, negate=True)
                v_nmax = (k.sem["ve"], k.sem["ve"].n)
                k.w("scalar", v_nmax)
                psl = step_of[(h, qt)] % NPN
                k.w("scalar", Pn_rel[psl])
                a_exp = k.act(k.Pn[psl][:, 0:L], k.Ss[sl][:, 0:L], AF.Exp,
                              bias=k.avec[sl][:, 8:9], scale=1.0,
                              accum_out=k.avec[sl][:, 9:10])
                Ss_rel[sl] = a_exp
                av_rel[sl] = a_exp
                k.w("vector", a_exp)
                v_rc = k.recip(k.avec[sl][:, 10:11], k.avec[sl][:, 9:10])
                k.w("vector", v_rc)
                v_norm = k.ts("vector", k.Pn[psl][:, 0:L], k.Pn[psl][:, 0:L],
                              k.avec[sl][:, 10:11], None, ALU.mult, ALU.bypass)
                return v_norm

            def make_transposes(h, qt, v_norm):
                def go():
                    nonlocal pt_ctr
                    i = g * QTG + qt
                    psl = step_of[(h, qt)] % NPN
                    k.w("tensor", v_norm)
                    pe_tr = None
                    for jp in range(0, i + 1, 8):
                        npk = min(8, i + 1 - jp)
                        tsl = pt_ctr % 3
                        pt_ctr += 1
                        k.w("tensor", ptp_rel[tsl])
                        for j in range(jp, jp + npk):
                            pe_tr = k.tr(k.psT[tsl][:, 128 * (j - jp):128 * (j - jp + 1)],
                                         k.Pn[psl][:, j * 128:(j + 1) * 128], inc=(j == jp + npk - 1))
                        k.w("scalar", pe_tr)
                        a_pt = k.act(PT_v[:, jp:jp + npk, qt * 128:(qt + 1) * 128],
                                     k.psT[tsl][:].rearrange("p (j q) -> p j q", j=8)[:, 0:npk, :],
                                     AF.Copy)
                        ptp_rel[tsl] = a_pt
                    Pn_rel[psl] = pe_tr
                return go

            def make_pv(h):
                def go():
                    kv = h // NREP
                    hsl = h % 2
                    k.w("tensor", ptp_rel[0])
                    k.w("tensor", ptp_rel[1])
                    k.w("tensor", ptp_rel[2])
                    k.w("tensor", ot_rel[hsl])
                    jmax = g * QTG + QTG - 1
                    k.mm(k.psB[hsl][:, 0:G], k.cst_id[:], k.Kc[:, kv * S: kv * S + G],
                         start=True, stop=True)
                    pe_pv = None
                    for j in range(jmax + 1):
                        jj = j - g * QTG
                        q0 = 0 if jj < 0 else jj * 128
                        pe_pv = k.mm(k.psB[hsl][:, q0:G],
                                     k.Vc[:, (j * NKV + kv) * 128:(j * NKV + kv + 1) * 128],
                                     PT_v[:, j, q0:G], start=(j == 0), stop=(j == jmax), inc=(j == jmax))
                    k.w("scalar", pe_pv)
                    k.w("scalar", otst_rel[hsl])
                    a_ot = k.act(k.otst[hsl][:], k.psB[hsl][:, 0:G], AF.Copy)
                    ot_rel[hsl] = a_ot
                    k.w("scalar", a_ot)
                    last = None
                    for tc in range(QTG):
                        last = k.dma("scalar", f"ot{hsl}", k.oT_scr[g % 2][tc][:, h * 128:(h + 1) * 128],
                                     k.otst[hsl][:, tc * 128:(tc + 1) * 128])
                    otst_rel[hsl] = last
                return go

            steps = [(h, qt) for h in range(NH) for qt in range(QTG)]
            step_of = {s: idx for idx, s in enumerate(steps)}
            for idx, (h, qt) in enumerate(steps):
                sl = idx % NSL
                i = g * QTG + qt
                L = (i + 1) * 128
                nck = (L + 511) // 512
                kv = h // NREP
                for kc in range(nck):
                    Lc = min(512, L - kc * 512)
                    csl = ck_ctr % 2
                    ck_ctr += 1
                    k.w("tensor", ssp_rel[csl])
                    if kc == 0:
                        k.mm(ckps[csl][:, 0:512], k.cst_id[:], k.Kc[:, kv * S: kv * S + 512],
                             start=True, stop=True)
                    diag = (kc == nck - 1)
                    pe_c = k.mm(ckps[csl][:, 0:Lc], k.Qg[:, h * G + qt * 128: h * G + (qt + 1) * 128],
                                k.Kc[:, kv * S + kc * 512: kv * S + kc * 512 + Lc],
                                start=True, stop=not diag, inc=not diag)
                    if diag:
                        pe_c = k.mm(ckps[csl][:, Lc - 128:Lc], k.cst_lt[:], k.cst_id[:],
                                    start=False, stop=True, inc=True)
                    k.w("vector", pe_c)
                    if kc == 0:
                        k.w("vector", Ss_rel[sl])
                        k.w("vector", av_rel[sl])
                    ve_c = k.ts("vector", k.Ss[sl][:, kc * 512:kc * 512 + Lc], ckps[csl][:, 0:Lc],
                                SCALE, None, ALU.mult, ALU.max,
                                accum_out=k.avec[sl][:, kc:kc + 1])
                    ssp_rel[csl] = ve_c
                qg_rel = (k.sem["pe"], k.sem["pe"].n)
                v_norm = emit_chain(h, qt, sl)
                deferred.append((idx + LAG_T, make_transposes(h, qt, v_norm)))
                if qt == QTG - 1:
                    deferred.append((idx + LAG_PV, make_pv(h)))
                while deferred and deferred[0][0] <= idx:
                    deferred.pop(0)[1]()
                if f_comps and idx % 2 == 0:
                    if f_loads:
                        f_loads.pop(0)()
                    f_comps.pop(0)()
            while deferred:
                deferred.pop(0)[1]()
            while f_comps:
                if f_loads:
                    f_loads.pop(0)()
                f_comps.pop(0)()
        NU = NDB * QTG
        t_loads = [make_load(cfg["NG"] - 1, u) for u in range(NU)]
        t_comps = [make_unit(cfg["NG"] - 1, u) for u in range(NU)]
        t_loads.pop(0)(); t_loads.pop(0)()
        while t_comps:
            if t_loads:
                t_loads.pop(0)()
            t_comps.pop(0)()

    def emit_b(self):
        k = self
        cfg = self.cfg
        S, D, I, G = (cfg[x] for x in ("S", "D", "I", "G"))
        C, ICN, DBH, QTG = (cfg[x] for x in ("C", "ICN", "DBH", "QTG"))
        Hv = k.Hbuf.ap().rearrange("p (ic t) -> p ic t", ic=ICN)
        xng_v = k.xng.ap().rearrange("p (c t) -> p c t", c=C)

        gw_rel = [None, None]
        uw_rel = [None, None]
        dw_rel = [None, None]
        gps_rel = [None, None]
        ups_rel = [None, None]
        fps_rel = [None, None]
        gt_rel = [None, None]
        ut_rel = [None, None]
        Ft_rel = [None, None]
        rB_rel = [None, None]
        oB_rel = [None, None]
        prev_down = None

        for g in range(cfg["NG"]):
            k.w("sync", prev_down)
            xng_ld = None
            for c in range(C):
                xng_ld = k.dma("sync", "xng", xng_v[:, c, :], k.xnT_scr[c][:, g * G:(g + 1) * G])
            for ic in range(ICN):
                wsl = ic % 2
                k.w("gpsimd", gw_rel[wsl])
                if ic < 2:
                    k.w("gpsimd", prev_down)
                g_ld = k.dma("gpsimd", f"gw{wsl}", k.gw_buf[wsl][:], k.gwblob[ic].rearrange("p c i -> p (c i)"),
                             max_dma_last_dim=8192)
                k.w("gpsimd", uw_rel[wsl])
                u_ld = k.dma("gpsimd", f"uw{wsl}", k.uw_buf[wsl][:], k.uwblob[ic].rearrange("p c i -> p (c i)"),
                             max_dma_last_dim=8192)
                k.w("tensor", g_ld)
                k.w("tensor", u_ld)
                k.w("tensor", xng_ld)
                psl = ic % 2
                k.w("tensor", gps_rel[psl])
                pe_g = None
                for c in range(C):
                    pe_g = k.mm(k.psB[psl][:, 0:G], k.gw_buf[wsl][:, c * 128:(c + 1) * 128],
                                xng_v[:, c, :], start=(c == 0), stop=(c == C - 1), inc=(c == C - 1))
                gw_rel[wsl] = pe_g
                upb = [k.psA[1], k.psC[0]][psl]
                k.w("tensor", ups_rel[psl])
                k.w("tensor", fps_rel[1] if psl == 0 else None)
                pe_u = None
                for c in range(C):
                    pe_u = k.mm(upb[:, 0:G], k.uw_buf[wsl][:, c * 128:(c + 1) * 128],
                                xng_v[:, c, :], start=(c == 0), stop=(c == C - 1), inc=(c == C - 1))
                uw_rel[wsl] = pe_u
                k.w("scalar", pe_g)
                k.w("scalar", gt_rel[psl])
                a_g = k.act(k.gtmp[psl][:], k.psB[psl][:, 0:G], AF.Silu, bias=0.0, scale=k.scals[:, 0:1])
                gps_rel[psl] = a_g
                k.w("scalar", pe_u)
                k.w("scalar", ut_rel[psl])
                a_u = k.act(k.utmp[psl][:], upb[:, 0:G], AF.Copy, bias=0.0, scale=k.scals[:, 1:2])
                ups_rel[psl] = a_u
                k.w("vector", a_g)
                k.w("vector", a_u)
                if ic < 2:
                    k.w("vector", prev_down)
                ve_h = k.tt("vector", Hv[:, ic, :], k.gtmp[psl][:], k.utmp[psl][:], ALU.mult)
                gt_rel[psl] = ve_h
                ut_rel[psl] = ve_h
            h_done = (k.sem["ve"], k.sem["ve"].n)
            gu_done = (k.sem["pe"], k.sem["pe"].n)
            for dbh in range(DBH):
                wsl = dbh % 2
                k.w("gpsimd", dw_rel[wsl])
                if dbh < 2:
                    k.w("gpsimd", gu_done)
                d_ld = k.dma("gpsimd", f"dw{wsl}", k.dw_buf[wsl][:], k.dwblob[dbh].rearrange("p ic o -> p (ic o)"),
                             max_dma_last_dim=8192)
                for tc in range(QTG):
                    psl = (dbh * QTG + tc) % 2
                    k.w("tensor", fps_rel[psl])
                    if psl == 1:
                        k.w("tensor", ups_rel[0])
                    k.w("tensor", d_ld)
                    k.w("tensor", h_done)
                    pe_f = None
                    for ic in range(ICN):
                        pe_f = k.mm(k.psA[psl][:, 0:256], Hv[:, ic, tc * 128:(tc + 1) * 128],
                                    k.dw_buf[wsl][:, ic * 256:(ic + 1) * 256],
                                    start=(ic == 0), stop=(ic == ICN - 1), inc=(ic == ICN - 1))
                    dw_rel[wsl] = pe_f
                    prev_down = pe_f
                    k.w("scalar", pe_f)
                    esl = (dbh * QTG + tc) % 2
                    k.w("scalar", Ft_rel[esl])
                    a_f = k.act(k.Ftmp[esl][:], k.psA[psl][:, 0:256], AF.Copy, bias=0.0, scale=k.scals[:, 2:3])
                    fps_rel[psl] = a_f
                    t0 = (g * QTG + tc) * 128
                    o0 = dbh * 256
                    k.w("sync", rB_rel[esl])
                    r_ld = k.dma("sync", f"rb{esl}", k.rB[esl][:], k.r_scr[t0:t0 + 128, o0:o0 + 256])
                    k.w("vector", r_ld)
                    k.w("vector", a_f)
                    k.w("vector", oB_rel[esl])
                    ve_o = k.tt("vector", k.oB[esl][:], k.Ftmp[esl][:], k.rB[esl][:], ALU.add)
                    Ft_rel[esl] = ve_o
                    rB_rel[esl] = ve_o
                    k.w("sync", ve_o)
                    oB_rel[esl] = k.dma("sync", f"ob{esl}", k.out[t0:t0 + 128, o0:o0 + 256], k.oB[esl][:])

    def finish(self):
        for q in ("sync", "scalar", "gpsimd"):
            for fam in sorted(self.qfams[q]):
                self.w(q, (self.dsems[fam], self.dsems[fam].n))

    def build(self):
        self.declare()
        self.emit_setup()
        for e in ("scalar", "vector", "tensor"):
            self.w(e, self.setup_ld)
        ph = self.cfg["phases"]
        if "A1" in ph:
            self.emit_a1()
        if "A2" in ph:
            self.barrier()
            self.emit_a2()
        if "B" in ph:
            self.barrier()
            self.emit_b()
        self.finish()
        nc = self.nc
        with nc.Block() as block:
            for e in ENGINES:
                getattr(block, e)(lambda eng, e=e: [f() for f in self.prog[e]])
        return nc


def host_blobs(cfg, wq, wk, wv, wo, gw, uw, dw):
    C, NH, NKV, ICN, DBH, WOB = (cfg[x] for x in ("C", "NH", "NKV", "ICN", "DBH", "WOB"))
    qkv = np.concatenate([np.asarray(wq), np.asarray(wk), np.asarray(wv)], axis=1)
    qkvblob = np.ascontiguousarray(qkv.reshape(C, 128, NH + 2 * NKV, 128).transpose(2, 1, 0, 3))
    woblob = np.ascontiguousarray(np.asarray(wo).reshape(C, 128, (C * 128) // 256, 256).transpose(2, 1, 0, 3))
    gwblob = np.ascontiguousarray(np.asarray(gw).reshape(ICN, 128, C, 128).transpose(0, 3, 2, 1))
    uwblob = np.ascontiguousarray(np.asarray(uw).reshape(ICN, 128, C, 128).transpose(0, 3, 2, 1))
    dwblob = np.ascontiguousarray(np.asarray(dw).reshape(DBH, 256, ICN, 128).transpose(0, 3, 2, 1))
    ident = np.eye(128, dtype=np.float32)
    lt = np.where(np.arange(128)[:, None] > np.arange(128)[None, :], np.float32(NEG_BIG), np.float32(0.0))
    cst = np.stack([ident, lt]).astype(np.float32)
    return dict(qkvblob=qkvblob, woblob=woblob, gwblob=gwblob, uwblob=uwblob, dwblob=dwblob, cst=cst)


def run_cores(cfg, inputs, n_cores, **kw):
    x = np.asarray(inputs["L0_attn_x"])
    blobs = host_blobs(cfg, inputs["L0_attn_wq"], inputs["L0_attn_wk"], inputs["L0_attn_wv"],
                       inputs["L0_attn_wo"], inputs["L0_ffn_gate_w"], inputs["L0_ffn_up_w"],
                       inputs["L0_ffn_down_w"])
    scal3 = np.stack([np.asarray(inputs["L0_ffn_gate_s"]).reshape(()),
                      np.asarray(inputs["L0_ffn_up_s"]).reshape(()),
                      np.asarray(inputs["L0_ffn_down_s"]).reshape(()),
                      np.float32(1e-5)]).astype(np.float32)
    kb = KB(cfg)
    nc = kb.build()
    in_maps = [dict(x=np.ascontiguousarray(x[b]), scal3=scal3, **blobs) for b in range(n_cores)]
    return run_bass_kernel_spmd(nc, in_maps, core_ids=list(range(n_cores)), **kw)


def kernel(**inputs):
    cfg = make_cfg()
    res = run_cores(cfg, inputs, 8)
    return np.stack([r["out"] for r in res.results], axis=0)


# version 16
# speedup vs baseline: 1.1152x; 1.1152x over previous
import numpy as np
import ml_dtypes
import concourse.bass as bass
import concourse.mybir as mybir
from concourse.bass_utils import run_bass_kernel_spmd

F32 = mybir.dt.float32
BF16 = mybir.dt.bfloat16
I32 = mybir.dt.int32
AF = mybir.ActivationFunctionType
ALU = mybir.AluOpType
AX = mybir.AxisListType

ENGINES = ("sync", "scalar", "vector", "tensor", "gpsimd")
NEG_BIG = -1.0e9


def make_cfg(S=2048, D=4096, I=11008, NH=32, NKV=8, G=512, phases=("A1", "A2", "B"), dbg=False):
    cfg = dict(S=S, D=D, I=I, NH=NH, NKV=NKV, G=G, phases=phases, dbg=dbg)
    cfg["C"] = D // 128
    cfg["NQT"] = S // 128
    cfg["NG"] = S // G
    cfg["QTG"] = G // 128
    cfg["ICN"] = I // 128
    cfg["DBH"] = D // 256
    cfg["WOB"] = D // 512
    cfg["NREP"] = NH // NKV
    cfg["SCALE"] = 128 ** -0.5
    assert S % G == 0 and G % 128 == 0 and D % 512 == 0 and I % 128 == 0
    return cfg


class Sem:
    def __init__(self, nc, name):
        self.h = nc.alloc_semaphore(name)
        self.name = name
        self.n = 0


class KB:
    def __init__(self, cfg):
        self.cfg = cfg
        nc = bass.Bass("TRN2", target_bir_lowering=False)
        self.nc = nc
        self.prog = {e: [] for e in ENGINES}
        self.lastw = {}
        self.sem = {k: Sem(nc, k) for k in ("pe", "ve", "ac", "gp", "ph")}
        self.dsems = {}
        self.qfams = {"gpsimd": set(), "sync": set(), "scalar": set()}

    def w(self, e, rel):
        if rel is None:
            return
        sem, val = rel
        if val is None or val <= 0:
            return
        key = (e, sem.name)
        if self.lastw.get(key, -1) >= val:
            return
        self.lastw[key] = val
        eng = getattr(self.nc, e)
        h = sem.h
        self.prog[e].append(lambda eng=eng, h=h, val=val: eng.wait_ge(h, val))

    def dma(self, q, fam, out, in_, **kw):
        if fam not in self.dsems:
            self.dsems[fam] = Sem(self.nc, "d_" + fam)
        self.qfams[q].add(fam)
        sem = self.dsems[fam]
        sem.n += 16
        eng = getattr(self.nc, q)
        h = sem.h
        self.prog[q].append(
            lambda eng=eng, h=h, out=out, in_=in_, kw=kw: eng.dma_start(out=out, in_=in_, **kw).then_inc(h, 16))
        return (sem, sem.n)

    def _emit(self, e, make, inc):
        eng = getattr(self.nc, e)
        if inc:
            sem = {"tensor": self.sem["pe"], "vector": self.sem["ve"],
                   "scalar": self.sem["ac"], "gpsimd": self.sem["gp"]}[e]
            sem.n += 1
            h = sem.h
            self.prog[e].append(lambda eng=eng, make=make, h=h: make(eng).then_inc(h, 1))
            return (sem, sem.n)
        self.prog[e].append(lambda eng=eng, make=make: make(eng))
        return None

    def mm(self, out, lhsT, rhs, start, stop, inc=False):
        return self._emit("tensor",
                          lambda eng, out=out, lhsT=lhsT, rhs=rhs, start=start, stop=stop:
                          eng.matmul(out, lhsT, rhs, start=start, stop=stop), inc)

    def tr(self, out, in_, inc=False):
        ident = self.cst_id
        return self._emit("tensor",
                          lambda eng, out=out, in_=in_, ident=ident: eng.transpose(out, in_, ident[:]), inc)

    def act(self, out, in_, func, bias=0.0, scale=1.0, accum_out=None, inc=True):
        return self._emit("scalar",
                          lambda eng, out=out, in_=in_, func=func, bias=bias, scale=scale, acc=accum_out:
                          eng.activation(out, in_, func, bias=bias, scale=scale, accum_out=acc), inc)

    def ts(self, e, out, in0, s1, s2, op0, op1, accum_out=None, inc=True):
        return self._emit(e,
                          lambda eng, out=out, in0=in0, s1=s1, s2=s2, op0=op0, op1=op1, acc=accum_out:
                          eng.tensor_scalar(out, in0, s1, s2, op0, op1, accum_out=acc), inc)

    def tt(self, e, out, in0, in1, op, inc=True):
        return self._emit(e,
                          lambda eng, out=out, in0=in0, in1=in1, op=op: eng.tensor_tensor(out, in0, in1, op), inc)

    def red(self, out, in_, op, negate=False, inc=True):
        return self._emit("vector",
                          lambda eng, out=out, in_=in_, op=op, neg=negate:
                          eng.tensor_reduce(out, in_, AX.X, op, negate=neg), inc)

    def recip(self, out, in_, inc=True):
        return self._emit("vector", lambda eng, out=out, in_=in_: eng.reciprocal(out, in_), inc)

    def copy(self, e, out, in_, inc=True):
        return self._emit(e, lambda eng, out=out, in_=in_: eng.tensor_copy(out, in_), inc)

    def bnstats(self, out, in_, inc=False):
        return self._emit("vector", lambda eng, out=out, in_=in_: eng.bn_stats(out, in_), inc)

    def bnaggr(self, out, in_, inc=True):
        return self._emit("vector", lambda eng, out=out, in_=in_: eng.bn_aggr(out, in_), inc)

    def barrier(self):
        ph = self.sem["ph"]
        for e in ENGINES:
            if e in self.qfams:
                for fam in sorted(self.qfams[e]):
                    self.w(e, (self.dsems[fam], self.dsems[fam].n))
            ph.n += 1
            eng = getattr(self.nc, e)
            h = ph.h
            self.prog[e].append(lambda eng=eng, h=h: eng.sem_inc(h, 1))
        for e in ENGINES:
            self.w(e, (ph, ph.n))

    def declare(self):
        nc, cfg = self.nc, self.cfg
        S, D, I, NH, NKV, G = (cfg[k] for k in ("S", "D", "I", "NH", "NKV", "G"))
        C, NQT, ICN, DBH, WOB = (cfg[k] for k in ("C", "NQT", "ICN", "DBH", "WOB"))
        scratch_kind = "ExternalOutput" if cfg["dbg"] else "Internal"

        self.x = nc.dram_tensor("x", [S, D], F32, kind="ExternalInput").ap()
        self.qkvblob = nc.dram_tensor("qkvblob", [NH + 2 * NKV, 128, C, 128], F32, kind="ExternalInput").ap()
        self.woblob = nc.dram_tensor("woblob", [D // 256, 128, C, 256], F32, kind="ExternalInput").ap()
        self.gwblob = nc.dram_tensor("gwblob", [ICN, 128, C, 128], I32, kind="ExternalInput").ap()
        self.uwblob = nc.dram_tensor("uwblob", [ICN, 128, C, 128], I32, kind="ExternalInput").ap()
        self.dwblob = nc.dram_tensor("dwblob", [DBH, 128, ICN, 256], I32, kind="ExternalInput").ap()
        self.scal3 = nc.dram_tensor("scal3", [4], F32, kind="ExternalInput").ap()
        self.cst = nc.dram_tensor("cst", [2, 128, 128], F32, kind="ExternalInput").ap()
        self.out = nc.dram_tensor("out", [S, D], F32, kind="ExternalOutput").ap()

        self.qT_scr = nc.dram_tensor("qT_scr", [NH, 128, S], BF16, kind=scratch_kind).ap()
        self.oT_scr = nc.dram_tensor("oT_scr", [2, G // 128, 128, NH * 128], BF16, kind="Internal").ap()
        self.kT_scr = nc.dram_tensor("kT_scr", [NKV, 128, S], BF16, kind=scratch_kind).ap()
        self.v_scr = nc.dram_tensor("v_scr", [NQT, 128, NKV * 128], BF16, kind=scratch_kind).ap()
        self.xnT_scr = nc.dram_tensor("xnT_scr", [C, 128, S], BF16, kind=scratch_kind).ap()
        self.r_scr = nc.dram_tensor("r_scr", [S, D], F32, kind=scratch_kind).ap()
        if cfg["dbg"]:
            self.dbg1 = nc.dram_tensor("dbg1", [128, 64], F32, kind="ExternalOutput").ap()
            self.dbg2 = nc.dram_tensor("dbg2", [128, D], F32, kind="ExternalOutput").ap()

        self.psA = [nc.alloc_psum_tensor(f"psA{i}", [128, 512], F32) for i in range(2)]
        self.psB = [nc.alloc_psum_tensor(f"psB{i}", [128, 512], F32) for i in range(2)]
        self.psC = [nc.alloc_psum_tensor(f"psC{i}", [128, 512], F32) for i in range(1)]
        self.psT = [nc.alloc_psum_tensor(f"psT{i}", [128, 1024], BF16) for i in range(3)]

        BASE, TOP = 16512, 229344
        cur = [BASE]

        def place(name, free_elems, dtype, at=None):
            esz = 2 if dtype == BF16 else 4
            off = cur[0] if at is None else at
            t = nc.alloc_sbuf_tensor_at(name, [128, free_elems], dtype, offset=off)
            if at is None:
                cur[0] = off + ((free_elems * esz + 31) // 32) * 32
            assert (off + free_elems * esz) <= TOP, f"SBUF overflow {name}"
            return t

        self.cst_id = place("cst_id", 128, BF16)
        self.cst_lt = place("cst_lt", 128, BF16)
        self.scals = place("scals", 16, F32)
        glob_end = cur[0]

        cur[0] = glob_end
        self.xnT = place("xnT", C * S, BF16)
        self.wbuf = [place(f"wbuf{i}", C * 128, BF16) for i in range(2)]
        self.xtile = [place(f"xtile{i}", D, F32) for i in range(2)]
        self.xn_tm = [place(f"xn_tm{i}", D, BF16) for i in range(2)]
        self.stg = [place(f"stg{i}", 512, BF16) for i in range(4)]
        self.vstg = [place(f"vstg{i}", 512, BF16) for i in range(2)]
        self.vtstg = [place(f"vtstg{i}", 512, BF16) for i in range(2)]
        self.bns = [place(f"bns{i}", 6 * (D // 512), F32) for i in range(2)]
        self.lnv = [place(f"lnv{i}", 8, F32) for i in range(2)]
        assert cur[0] <= TOP, f"A1 overflow {cur[0]}"

        cur[0] = glob_end
        self.Qg = place("Qg", NH * G, BF16)
        self.Kc = place("Kc", NKV * S, BF16)
        self.Vc = place("Vc", NQT * NKV * 128, BF16)
        self.avec = [place(f"avec{i}", 16, F32) for i in range(3)]
        self.xA = [place(f"xA{i}", 256, F32) for i in range(2)]
        self.rA = [place(f"rA{i}", 256, F32) for i in range(2)]
        self.Ss = [place(f"Ss{i}", S, F32) for i in range(2)]
        self.Pn = [place(f"Pn{i}", S, BF16) for i in range(4)]
        self.PT = place("PT", NQT * G, BF16)
        self.ogt = [place(f"ogt{i}", NH * 128, BF16) for i in range(2)]
        self.wo_buf = [place(f"wo_buf{i}", C * 256, BF16) for i in range(2)]
        self.otst = [place(f"otst{i}", G, BF16) for i in range(2)]
        assert cur[0] <= TOP, f"A2 overflow {cur[0]}"

        cur[0] = glob_end
        self.Hbuf = place("Hbuf", ICN * G, BF16)
        self.gtmp = [place(f"gtmp{i}", G, BF16) for i in range(2)]
        self.utmp = [place(f"utmp{i}", G, BF16) for i in range(2)]
        self.Ftmp = [place(f"Ftmp{i}", 256, F32) for i in range(2)]
        self.rB = [place(f"rB{i}", 256, F32) for i in range(2)]
        self.oB = [place(f"oB{i}", 256, F32) for i in range(2)]
        dw_ov = cur[0]
        self.xng = place("xng", C * G, BF16)
        self.gw_buf = [place(f"gw_buf{i}", C * 128, BF16) for i in range(2)]
        self.uw_buf = [place(f"uw_buf{i}", C * 128, BF16) for i in range(2)]
        b_end = cur[0]
        cur[0] = dw_ov
        self.dw_buf = [place(f"dw_buf{i}", ICN * 256, BF16) for i in range(2)]
        assert max(b_end, cur[0]) <= TOP, f"B overflow {max(b_end, cur[0])}"

    def emit_setup(self):
        k = self
        k.dma("gpsimd", "setup", k.cst_id[:], k.cst[0])
        k.dma("gpsimd", "setup", k.cst_lt[:], k.cst[1])
        for i in range(4):
            last = k.dma("gpsimd", "setup", k.scals[:, i:i + 1], k.scal3[None, i:i + 1].broadcast_to([128, 1]))
        self.setup_ld = last

    def emit_a1(self):
        k = self
        cfg = self.cfg
        S, D, NH, NKV, C, NQT = (cfg[x] for x in ("S", "D", "NH", "NKV", "C", "NQT"))
        xnT_v = k.xnT.ap().rearrange("p (c t) -> p c t", c=C)

        xt_rel = [None, None]
        xn_rel = [None, None]
        trp_rel = [None, None]
        lnv_rel = [None, None]
        nb = D // 512
        tr_ctr = 0
        for t in range(NQT):
            sl = t % 2
            k.w("sync", xt_rel[sl])
            x_ld = k.dma("sync", f"xt{sl}", k.xtile[sl][:], k.x[t * 128:(t + 1) * 128, :])
            k.w("vector", x_ld)
            v_bns = None
            for b in range(nb):
                v_bns = k.bnstats(k.bns[sl][:, 6 * b:6 * (b + 1)], k.xtile[sl][:, 512 * b:512 * (b + 1)],
                                  inc=(b == nb - 1))
            k.w("vector", v_bns)
            k.w("vector", lnv_rel[sl])
            k.bnaggr(k.lnv[sl][:, 0:2], k.bns[sl][:])
            v_aggr = (k.sem["ve"], k.sem["ve"].n)
            k.w("scalar", v_aggr)
            a_std = k.act(k.lnv[sl][:, 2:3], k.lnv[sl][:, 1:2], AF.Sqrt, bias=k.scals[:, 3:4], scale=1.0)
            lnv_rel[sl] = a_std
            k.w("vector", a_std)
            v_rc = k.recip(k.lnv[sl][:, 3:4], k.lnv[sl][:, 2:3])
            k.w("vector", v_rc)
            v_nm = k.ts("vector", k.lnv[sl][:, 4:5], k.lnv[sl][:, 0:1], k.lnv[sl][:, 3:4], -1.0,
                        ALU.mult, ALU.mult)
            k.w("vector", v_nm)
            k.w("vector", xn_rel[sl])
            v_norm = k.ts("vector", k.xn_tm[sl][:], k.xtile[sl][:], k.lnv[sl][:, 3:4], k.lnv[sl][:, 4:5],
                          ALU.mult, ALU.add)
            xt_rel[sl] = v_norm
            k.w("tensor", v_norm)
            for cp in range(0, C, 4):
                npk = min(4, C - cp)
                tsl = tr_ctr % 2
                tr_ctr += 1
                k.w("tensor", trp_rel[tsl])
                pe_c = None
                for j in range(npk):
                    pe_c = k.tr(k.psT[tsl][:, 128 * j:128 * (j + 1)],
                                k.xn_tm[sl][:, 128 * (cp + j):128 * (cp + j + 1)],
                                inc=(j == npk - 1))
                xn_rel[sl] = pe_c
                k.w("vector", pe_c)
                trp_rel[tsl] = k.copy("vector", xnT_v[:, cp:cp + npk, t * 128:(t + 1) * 128],
                                      k.psT[tsl][:].rearrange("p (c t) -> p c t", c=8)[:, 0:npk, :])
            if t == 0 and self.cfg["dbg"]:
                k.w("sync", (k.sem["ve"], k.sem["ve"].n))
                k.dma("sync", "dbgf", k.dbg1[:, 0:16], k.scals[:])
                k.dma("sync", "dbgf", k.dbg1[:, 16:24], k.lnv[0][:])
                k.dma("sync", "dbgf", k.dbg1[:, 24:24 + 6 * (D // 512)], k.bns[0][:])
                k.w("gpsimd", (k.sem["ve"], k.sem["ve"].n))
                k.dma("gpsimd", "dbgf2", k.dbg2[:, 0:D], k.xn_tm[0][:])
        ln_done = (k.sem["ve"], k.sem["ve"].n)
        k.w("sync", ln_done)
        for c in range(C):
            k.dma("sync", "xnw", k.xnT_scr[c], xnT_v[:, c, :])

        k.w("tensor", ln_done)
        NOC = NH + 2 * NKV
        w_rel = [None, None]
        ps_rel = [None, None]
        st_rel = [None] * 4
        vs_rel = [None, None]
        vt_rel = [None] * 2
        ntc = S // 512
        step = 0
        for oc in range(NOC):
            wsl = oc % 2
            k.w("gpsimd", w_rel[wsl])
            w_ld = k.dma("gpsimd", f"wb{wsl}", k.wbuf[wsl][:], k.qkvblob[oc].rearrange("p c i -> p (c i)"),
                         max_dma_last_dim=8192)
            k.w("tensor", w_ld)
            is_v = oc >= NH + NKV
            for tc in range(ntc):
                psl = step % 2
                step += 1
                k.w("tensor", ps_rel[psl])
                pe_c = None
                for c in range(C):
                    pe_c = k.mm(k.psB[psl][:], k.wbuf[wsl][:, c * 128:(c + 1) * 128],
                                xnT_v[:, c, tc * 512:(tc + 1) * 512],
                                start=(c == 0), stop=(c == C - 1), inc=(c == C - 1))
                w_rel[wsl] = pe_c
                k.w("scalar", pe_c)
                if not is_v:
                    ssl = step % 4
                    k.w("scalar", st_rel[ssl])
                    a_c = k.act(k.stg[ssl][:], k.psB[psl][:], AF.Copy)
                    ps_rel[psl] = a_c
                    k.w("sync", a_c)
                    if oc < NH:
                        dst = k.qT_scr[oc][:, tc * 512:(tc + 1) * 512]
                    else:
                        dst = k.kT_scr[oc - NH][:, tc * 512:(tc + 1) * 512]
                    st_rel[ssl] = k.dma("sync", f"stg{ssl}", dst, k.stg[ssl][:])
                else:
                    vsl = step % 2
                    k.w("scalar", vs_rel[vsl])
                    a_c = k.act(k.vstg[vsl][:], k.psB[psl][:], AF.Copy)
                    ps_rel[psl] = a_c
                    k.w("tensor", a_c)
                    tsl = step % 2
                    k.w("tensor", trp_rel[tsl])
                    pe_t = None
                    for j in range(4):
                        pe_t = k.tr(k.psT[tsl][:, 128 * j:128 * (j + 1)],
                                    k.vstg[vsl][:, 128 * j:128 * (j + 1)], inc=(j == 3))
                    vs_rel[vsl] = pe_t
                    k.w("scalar", pe_t)
                    k.w("scalar", vt_rel[tsl])
                    a_t = k.act(k.vtstg[tsl][:], k.psT[tsl][:, 0:512], AF.Copy)
                    trp_rel[tsl] = a_t
                    k.w("sync", a_t)
                    kv = oc - NH - NKV
                    last = None
                    for j in range(4):
                        tt_ = tc * 4 + j
                        last = k.dma("sync", f"vt{tsl}", k.v_scr[tt_][:, kv * 128:(kv + 1) * 128],
                                     k.vtstg[tsl][:, 128 * j:128 * (j + 1)])
                    vt_rel[tsl] = last

    def emit_a2(self):
        k = self
        cfg = self.cfg
        S, NH, NKV, G = (cfg[x] for x in ("S", "NH", "NKV", "G"))
        C, NQT, QTG, NREP = (cfg[x] for x in ("C", "NQT", "QTG", "NREP"))
        D = cfg["D"]
        NDB = D // 256
        SCALE = cfg["SCALE"]
        PT_v = k.PT.ap().rearrange("p (j q) -> p j q", j=NQT)
        NSL = 2
        NPN = 4
        LAG_T = 3
        LAG_PV = 5

        ckps = [k.psA[0], k.psA[1]]
        psF = k.psC[0]
        ssp_rel = [None, None]
        Ss_rel = [None] * NSL
        Pn_rel = [None] * NPN
        av_rel = [None] * NSL
        ptp_rel = [None, None, None]
        ot_rel = [None, None]
        otst_rel = [None, None]
        qg_rel = None
        wo_rel = [None, None]
        wo_lds = [None, None]
        og_lds = [None, None]
        ogt_rel = [None, None]
        psF_rel = None
        xA_rel = [None, None]
        rA_rel = [None, None]
        units_done_pe = None
        ck_ctr = 0
        pt_ctr = 0
        un_ctr = 0

        def make_load(gm1, u, ot_snap):
            def go():
                db = u // QTG
                tc = u % QTG
                osl = u % 2
                wsl = db % 2
                if tc == 0:
                    k.w("gpsimd", wo_rel[wsl])
                    wo_lds[wsl] = k.dma("gpsimd", "wo", k.wo_buf[wsl][:],
                                        k.woblob[db].rearrange("p c j -> p (c j)"),
                                        max_dma_last_dim=8192)
                k.w("sync", ogt_rel[osl])
                for r in ot_snap:
                    k.w("sync", r)
                og_lds[osl] = k.dma("sync", f"og{osl}", k.ogt[osl][:], k.oT_scr[gm1 % 2][tc])
            return go

        def make_unit(gm1, u):
            def go():
                nonlocal psF_rel, un_ctr, units_done_pe
                db = u // QTG
                tc = u % QTG
                osl = u % 2
                wsl = db % 2
                un_ctr += 1
                k.w("tensor", wo_lds[wsl])
                k.w("tensor", og_lds[osl])
                k.w("tensor", psF_rel)
                pe_c = None
                for c in range(C):
                    pe_c = k.mm(psF[:, 0:256], k.ogt[osl][:, c * 128:(c + 1) * 128],
                                k.wo_buf[wsl][:, c * 256:(c + 1) * 256],
                                start=(c == 0), stop=(c == C - 1), inc=(c == C - 1))
                wo_rel[wsl] = pe_c
                ogt_rel[osl] = pe_c
                units_done_pe = pe_c
                xsl = un_ctr % 2
                k.w("sync", xA_rel[xsl])
                t0 = (gm1 * QTG + tc) * 128
                o0 = db * 256
                x_ld = k.dma("sync", f"xa{xsl}", k.xA[xsl][:], k.x[t0:t0 + 128, o0:o0 + 256])
                k.w("vector", x_ld)
                k.w("vector", pe_c)
                k.w("vector", rA_rel[xsl])
                ve_r = k.tt("vector", k.rA[xsl][:], psF[:, 0:256], k.xA[xsl][:], ALU.add)
                psF_rel = ve_r
                xA_rel[xsl] = ve_r
                k.w("sync", ve_r)
                rA_rel[xsl] = k.dma("sync", f"ra{xsl}", k.r_scr[t0:t0 + 128, o0:o0 + 256], k.rA[xsl][:])
            return go

        for g in range(cfg["NG"]):
            k.w("scalar", qg_rel)
            q_ld = None
            kv_ld = None
            for h in range(NH):
                q_ld = k.dma("scalar", "qg", k.Qg[:, h * G:(h + 1) * G], k.qT_scr[h][:, g * G:(g + 1) * G])
            for kv in range(NKV):
                kv_ld = k.dma("scalar", "kv", k.Kc[:, kv * S + g * G: kv * S + (g + 1) * G],
                              k.kT_scr[kv][:, g * G:(g + 1) * G])
            for j in range(QTG):
                tt_ = g * QTG + j
                kv_ld = k.dma("scalar", "kv", k.Vc[:, tt_ * NKV * 128:(tt_ + 1) * NKV * 128], k.v_scr[tt_])
            k.w("tensor", q_ld)
            k.w("tensor", kv_ld)
            k.w("scalar", units_done_pe)

            deferred = []
            NU = NDB * QTG
            ot_snap = [(k.dsems[f], k.dsems[f].n) for f in ("ot0", "ot1") if f in k.dsems]
            if g > 0:
                f_loads = [make_load(g - 1, u, ot_snap) for u in range(NU)]
                f_comps = [make_unit(g - 1, u) for u in range(NU)]
                f_loads.pop(0)(); f_loads.pop(0)()
            else:
                f_loads, f_comps = [], []

            def emit_chain(h, qt, sl):
                i = g * QTG + qt
                L = (i + 1) * 128
                nck = (L + 511) // 512
                k.w("vector", (k.sem["ve"], k.sem["ve"].n))
                k.red(k.avec[sl][:, 8:9], k.avec[sl][:, 0:nck], ALU.max, negate=True)
                v_nmax = (k.sem["ve"], k.sem["ve"].n)
                k.w("scalar", v_nmax)
                psl = step_of[(h, qt)] % NPN
                k.w("scalar", Pn_rel[psl])
                a_exp = k.act(k.Pn[psl][:, 0:L], k.Ss[sl][:, 0:L], AF.Exp,
                              bias=k.avec[sl][:, 8:9], scale=1.0,
                              accum_out=k.avec[sl][:, 9:10])
                Ss_rel[sl] = a_exp
                av_rel[sl] = a_exp
                k.w("vector", a_exp)
                v_rc = k.recip(k.avec[sl][:, 10:11], k.avec[sl][:, 9:10])
                k.w("vector", v_rc)
                v_norm = k.ts("vector", k.Pn[psl][:, 0:L], k.Pn[psl][:, 0:L],
                              k.avec[sl][:, 10:11], None, ALU.mult, ALU.bypass)
                return v_norm

            def make_transposes(h, qt, v_norm):
                def go():
                    nonlocal pt_ctr
                    i = g * QTG + qt
                    psl = step_of[(h, qt)] % NPN
                    k.w("tensor", v_norm)
                    pe_tr = None
                    for jp in range(0, i + 1, 8):
                        npk = min(8, i + 1 - jp)
                        tsl = pt_ctr % 3
                        pt_ctr += 1
                        k.w("tensor", ptp_rel[tsl])
                        for j in range(jp, jp + npk):
                            pe_tr = k.tr(k.psT[tsl][:, 128 * (j - jp):128 * (j - jp + 1)],
                                         k.Pn[psl][:, j * 128:(j + 1) * 128], inc=(j == jp + npk - 1))
                        k.w("scalar", pe_tr)
                        a_pt = k.act(PT_v[:, jp:jp + npk, qt * 128:(qt + 1) * 128],
                                     k.psT[tsl][:].rearrange("p (j q) -> p j q", j=8)[:, 0:npk, :],
                                     AF.Copy)
                        ptp_rel[tsl] = a_pt
                    Pn_rel[psl] = pe_tr
                return go

            def make_pv(h):
                def go():
                    kv = h // NREP
                    hsl = h % 2
                    k.w("tensor", ptp_rel[0])
                    k.w("tensor", ptp_rel[1])
                    k.w("tensor", ptp_rel[2])
                    k.w("tensor", ot_rel[hsl])
                    jmax = g * QTG + QTG - 1
                    k.mm(k.psB[hsl][:, 0:G], k.cst_id[:], k.Kc[:, kv * S: kv * S + G],
                         start=True, stop=True)
                    pe_pv = None
                    for j in range(jmax + 1):
                        jj = j - g * QTG
                        q0 = 0 if jj < 0 else jj * 128
                        pe_pv = k.mm(k.psB[hsl][:, q0:G],
                                     k.Vc[:, (j * NKV + kv) * 128:(j * NKV + kv + 1) * 128],
                                     PT_v[:, j, q0:G], start=(j == 0), stop=(j == jmax), inc=(j == jmax))
                    k.w("scalar", pe_pv)
                    k.w("scalar", otst_rel[hsl])
                    a_ot = k.act(k.otst[hsl][:], k.psB[hsl][:, 0:G], AF.Copy)
                    ot_rel[hsl] = a_ot
                    k.w("scalar", a_ot)
                    last = None
                    for tc in range(QTG):
                        last = k.dma("scalar", f"ot{hsl}", k.oT_scr[g % 2][tc][:, h * 128:(h + 1) * 128],
                                     k.otst[hsl][:, tc * 128:(tc + 1) * 128])
                    otst_rel[hsl] = last
                return go

            steps = [(h, qt) for h in range(NH) for qt in range(QTG)]
            step_of = {s: idx for idx, s in enumerate(steps)}
            for idx, (h, qt) in enumerate(steps):
                sl = idx % NSL
                i = g * QTG + qt
                L = (i + 1) * 128
                nck = (L + 511) // 512
                kv = h // NREP
                for kc in range(nck):
                    Lc = min(512, L - kc * 512)
                    csl = ck_ctr % 2
                    ck_ctr += 1
                    k.w("tensor", ssp_rel[csl])
                    if kc == 0:
                        k.mm(ckps[csl][:, 0:512], k.cst_id[:], k.Kc[:, kv * S: kv * S + 512],
                             start=True, stop=True)
                    diag = (kc == nck - 1)
                    pe_c = k.mm(ckps[csl][:, 0:Lc], k.Qg[:, h * G + qt * 128: h * G + (qt + 1) * 128],
                                k.Kc[:, kv * S + kc * 512: kv * S + kc * 512 + Lc],
                                start=True, stop=not diag, inc=not diag)
                    if diag:
                        pe_c = k.mm(ckps[csl][:, Lc - 128:Lc], k.cst_lt[:], k.cst_id[:],
                                    start=False, stop=True, inc=True)
                    k.w("vector", pe_c)
                    if kc == 0:
                        k.w("vector", Ss_rel[sl])
                        k.w("vector", av_rel[sl])
                    ve_c = k.ts("vector", k.Ss[sl][:, kc * 512:kc * 512 + Lc], ckps[csl][:, 0:Lc],
                                SCALE, None, ALU.mult, ALU.max,
                                accum_out=k.avec[sl][:, kc:kc + 1])
                    ssp_rel[csl] = ve_c
                qg_rel = (k.sem["pe"], k.sem["pe"].n)
                v_norm = emit_chain(h, qt, sl)
                deferred.append((idx + LAG_T, make_transposes(h, qt, v_norm)))
                if qt == QTG - 1:
                    deferred.append((idx + LAG_PV, make_pv(h)))
                while deferred and deferred[0][0] <= idx:
                    deferred.pop(0)[1]()
                if f_comps and idx % 2 == 0:
                    if f_loads:
                        f_loads.pop(0)()
                    f_comps.pop(0)()
            while deferred:
                deferred.pop(0)[1]()
            while f_comps:
                if f_loads:
                    f_loads.pop(0)()
                f_comps.pop(0)()
        NU = NDB * QTG
        ot_snap = [(k.dsems[f], k.dsems[f].n) for f in ("ot0", "ot1") if f in k.dsems]
        t_loads = [make_load(cfg["NG"] - 1, u, ot_snap) for u in range(NU)]
        t_comps = [make_unit(cfg["NG"] - 1, u) for u in range(NU)]
        t_loads.pop(0)(); t_loads.pop(0)()
        while t_comps:
            if t_loads:
                t_loads.pop(0)()
            t_comps.pop(0)()

    def emit_b(self):
        k = self
        cfg = self.cfg
        S, D, I, G = (cfg[x] for x in ("S", "D", "I", "G"))
        C, ICN, DBH, QTG = (cfg[x] for x in ("C", "ICN", "DBH", "QTG"))
        Hv = k.Hbuf.ap().rearrange("p (ic t) -> p ic t", ic=ICN)
        xng_v = k.xng.ap().rearrange("p (c t) -> p c t", c=C)

        gw_rel = [None, None]
        uw_rel = [None, None]
        dw_rel = [None, None]
        gps_rel = [None, None]
        ups_rel = [None, None]
        fps_rel = [None, None]
        gt_rel = [None, None]
        ut_rel = [None, None]
        Ft_rel = [None, None]
        rB_rel = [None, None]
        oB_rel = [None, None]
        prev_down = None

        for g in range(cfg["NG"]):
            k.w("sync", prev_down)
            xng_ld = None
            for c in range(C):
                xng_ld = k.dma("sync", "xng", xng_v[:, c, :], k.xnT_scr[c][:, g * G:(g + 1) * G])
            for ic in range(ICN):
                wsl = ic % 2
                k.w("gpsimd", gw_rel[wsl])
                if ic < 2:
                    k.w("gpsimd", prev_down)
                g_ld = k.dma("gpsimd", f"gw{wsl}", k.gw_buf[wsl][:], k.gwblob[ic].rearrange("p c i -> p (c i)"),
                             max_dma_last_dim=8192)
                k.w("gpsimd", uw_rel[wsl])
                u_ld = k.dma("gpsimd", f"uw{wsl}", k.uw_buf[wsl][:], k.uwblob[ic].rearrange("p c i -> p (c i)"),
                             max_dma_last_dim=8192)
                k.w("tensor", g_ld)
                k.w("tensor", u_ld)
                k.w("tensor", xng_ld)
                psl = ic % 2
                k.w("tensor", gps_rel[psl])
                pe_g = None
                for c in range(C):
                    pe_g = k.mm(k.psB[psl][:, 0:G], k.gw_buf[wsl][:, c * 128:(c + 1) * 128],
                                xng_v[:, c, :], start=(c == 0), stop=(c == C - 1), inc=(c == C - 1))
                gw_rel[wsl] = pe_g
                upb = [k.psA[1], k.psC[0]][psl]
                k.w("tensor", ups_rel[psl])
                k.w("tensor", fps_rel[1] if psl == 0 else None)
                pe_u = None
                for c in range(C):
                    pe_u = k.mm(upb[:, 0:G], k.uw_buf[wsl][:, c * 128:(c + 1) * 128],
                                xng_v[:, c, :], start=(c == 0), stop=(c == C - 1), inc=(c == C - 1))
                uw_rel[wsl] = pe_u
                k.w("scalar", pe_g)
                k.w("scalar", gt_rel[psl])
                a_g = k.act(k.gtmp[psl][:], k.psB[psl][:, 0:G], AF.Silu, bias=0.0, scale=k.scals[:, 0:1])
                gps_rel[psl] = a_g
                k.w("scalar", pe_u)
                k.w("scalar", ut_rel[psl])
                a_u = k.act(k.utmp[psl][:], upb[:, 0:G], AF.Copy, bias=0.0, scale=k.scals[:, 1:2])
                ups_rel[psl] = a_u
                k.w("vector", a_g)
                k.w("vector", a_u)
                if ic < 2:
                    k.w("vector", prev_down)
                ve_h = k.tt("vector", Hv[:, ic, :], k.gtmp[psl][:], k.utmp[psl][:], ALU.mult)
                gt_rel[psl] = ve_h
                ut_rel[psl] = ve_h
            h_done = (k.sem["ve"], k.sem["ve"].n)
            gu_done = (k.sem["pe"], k.sem["pe"].n)
            for dbh in range(DBH):
                wsl = dbh % 2
                k.w("gpsimd", dw_rel[wsl])
                if dbh < 2:
                    k.w("gpsimd", gu_done)
                d_ld = k.dma("gpsimd", f"dw{wsl}", k.dw_buf[wsl][:], k.dwblob[dbh].rearrange("p ic o -> p (ic o)"),
                             max_dma_last_dim=8192)
                for tc in range(QTG):
                    psl = (dbh * QTG + tc) % 2
                    k.w("tensor", fps_rel[psl])
                    if psl == 1:
                        k.w("tensor", ups_rel[0])
                    k.w("tensor", d_ld)
                    k.w("tensor", h_done)
                    pe_f = None
                    for ic in range(ICN):
                        pe_f = k.mm(k.psA[psl][:, 0:256], Hv[:, ic, tc * 128:(tc + 1) * 128],
                                    k.dw_buf[wsl][:, ic * 256:(ic + 1) * 256],
                                    start=(ic == 0), stop=(ic == ICN - 1), inc=(ic == ICN - 1))
                    dw_rel[wsl] = pe_f
                    prev_down = pe_f
                    k.w("scalar", pe_f)
                    esl = (dbh * QTG + tc) % 2
                    k.w("scalar", Ft_rel[esl])
                    a_f = k.act(k.Ftmp[esl][:], k.psA[psl][:, 0:256], AF.Copy, bias=0.0, scale=k.scals[:, 2:3])
                    fps_rel[psl] = a_f
                    t0 = (g * QTG + tc) * 128
                    o0 = dbh * 256
                    k.w("sync", rB_rel[esl])
                    r_ld = k.dma("sync", f"rb{esl}", k.rB[esl][:], k.r_scr[t0:t0 + 128, o0:o0 + 256])
                    k.w("vector", r_ld)
                    k.w("vector", a_f)
                    k.w("vector", oB_rel[esl])
                    ve_o = k.tt("vector", k.oB[esl][:], k.Ftmp[esl][:], k.rB[esl][:], ALU.add)
                    Ft_rel[esl] = ve_o
                    rB_rel[esl] = ve_o
                    k.w("sync", ve_o)
                    oB_rel[esl] = k.dma("sync", f"ob{esl}", k.out[t0:t0 + 128, o0:o0 + 256], k.oB[esl][:])

    def finish(self):
        for q in ("sync", "scalar", "gpsimd"):
            for fam in sorted(self.qfams[q]):
                self.w(q, (self.dsems[fam], self.dsems[fam].n))

    def build(self):
        self.declare()
        self.emit_setup()
        for e in ("scalar", "vector", "tensor"):
            self.w(e, self.setup_ld)
        ph = self.cfg["phases"]
        if "A1" in ph:
            self.emit_a1()
        if "A2" in ph:
            self.barrier()
            self.emit_a2()
        if "B" in ph:
            self.barrier()
            self.emit_b()
        self.finish()
        nc = self.nc
        with nc.Block() as block:
            for e in ENGINES:
                getattr(block, e)(lambda eng, e=e: [f() for f in self.prog[e]])
        return nc


def host_blobs(cfg, wq, wk, wv, wo, gw, uw, dw):
    C, NH, NKV, ICN, DBH, WOB = (cfg[x] for x in ("C", "NH", "NKV", "ICN", "DBH", "WOB"))
    qkv = np.concatenate([np.asarray(wq), np.asarray(wk), np.asarray(wv)], axis=1)
    qkvblob = np.ascontiguousarray(qkv.reshape(C, 128, NH + 2 * NKV, 128).transpose(2, 1, 0, 3))
    woblob = np.ascontiguousarray(np.asarray(wo).reshape(C, 128, (C * 128) // 256, 256).transpose(2, 1, 0, 3))
    gwblob = np.ascontiguousarray(np.asarray(gw).reshape(ICN, 128, C, 128).transpose(0, 3, 2, 1))
    uwblob = np.ascontiguousarray(np.asarray(uw).reshape(ICN, 128, C, 128).transpose(0, 3, 2, 1))
    dwblob = np.ascontiguousarray(np.asarray(dw).reshape(DBH, 256, ICN, 128).transpose(0, 3, 2, 1))
    ident = np.eye(128, dtype=np.float32)
    lt = np.where(np.arange(128)[:, None] > np.arange(128)[None, :], np.float32(NEG_BIG), np.float32(0.0))
    cst = np.stack([ident, lt]).astype(np.float32)
    return dict(qkvblob=qkvblob, woblob=woblob, gwblob=gwblob, uwblob=uwblob, dwblob=dwblob, cst=cst)


def run_cores(cfg, inputs, n_cores, **kw):
    x = np.asarray(inputs["L0_attn_x"])
    blobs = host_blobs(cfg, inputs["L0_attn_wq"], inputs["L0_attn_wk"], inputs["L0_attn_wv"],
                       inputs["L0_attn_wo"], inputs["L0_ffn_gate_w"], inputs["L0_ffn_up_w"],
                       inputs["L0_ffn_down_w"])
    scal3 = np.stack([np.asarray(inputs["L0_ffn_gate_s"]).reshape(()),
                      np.asarray(inputs["L0_ffn_up_s"]).reshape(()),
                      np.asarray(inputs["L0_ffn_down_s"]).reshape(()),
                      np.float32(1e-5)]).astype(np.float32)
    kb = KB(cfg)
    nc = kb.build()
    in_maps = [dict(x=np.ascontiguousarray(x[b]), scal3=scal3, **blobs) for b in range(n_cores)]
    return run_bass_kernel_spmd(nc, in_maps, core_ids=list(range(n_cores)), **kw)


def kernel(**inputs):
    cfg = make_cfg()
    res = run_cores(cfg, inputs, 8)
    return np.stack([r["out"] for r in res.results], axis=0)


# version 17
# speedup vs baseline: 1.2520x; 1.1226x over previous
import numpy as np
import ml_dtypes
import concourse.bass as bass
import concourse.mybir as mybir
from concourse.bass_utils import run_bass_kernel_spmd

F32 = mybir.dt.float32
BF16 = mybir.dt.bfloat16
I32 = mybir.dt.int32
AF = mybir.ActivationFunctionType
ALU = mybir.AluOpType
AX = mybir.AxisListType

ENGINES = ("sync", "scalar", "vector", "tensor", "gpsimd")
NEG_BIG = -1.0e9


def make_cfg(S=2048, D=4096, I=11008, NH=32, NKV=8, G=512, phases=("A1", "A2", "B"), dbg=False):
    cfg = dict(S=S, D=D, I=I, NH=NH, NKV=NKV, G=G, phases=phases, dbg=dbg)
    cfg["C"] = D // 128
    cfg["NQT"] = S // 128
    cfg["NG"] = S // G
    cfg["QTG"] = G // 128
    cfg["ICN"] = I // 128
    cfg["DBH"] = D // 256
    cfg["WOB"] = D // 512
    cfg["NREP"] = NH // NKV
    cfg["SCALE"] = 128 ** -0.5
    assert S % G == 0 and G % 128 == 0 and D % 512 == 0 and I % 128 == 0
    return cfg


class Sem:
    def __init__(self, nc, name):
        self.h = nc.alloc_semaphore(name)
        self.name = name
        self.n = 0


class KB:
    def __init__(self, cfg):
        self.cfg = cfg
        nc = bass.Bass("TRN2", target_bir_lowering=False)
        self.nc = nc
        self.prog = {e: [] for e in ENGINES}
        self.lastw = {}
        self.sem = {k: Sem(nc, k) for k in ("pe", "ve", "ac", "gp", "ph")}
        self.dsems = {}
        self.qfams = {"gpsimd": set(), "sync": set(), "scalar": set()}

    def w(self, e, rel):
        if rel is None:
            return
        sem, val = rel
        if val is None or val <= 0:
            return
        key = (e, sem.name)
        if self.lastw.get(key, -1) >= val:
            return
        self.lastw[key] = val
        eng = getattr(self.nc, e)
        h = sem.h
        self.prog[e].append(lambda eng=eng, h=h, val=val: eng.wait_ge(h, val))

    def dma(self, q, fam, out, in_, **kw):
        if fam not in self.dsems:
            self.dsems[fam] = Sem(self.nc, "d_" + fam)
        self.qfams[q].add(fam)
        sem = self.dsems[fam]
        sem.n += 16
        eng = getattr(self.nc, q)
        h = sem.h
        self.prog[q].append(
            lambda eng=eng, h=h, out=out, in_=in_, kw=kw: eng.dma_start(out=out, in_=in_, **kw).then_inc(h, 16))
        return (sem, sem.n)

    def _emit(self, e, make, inc):
        eng = getattr(self.nc, e)
        if inc:
            sem = {"tensor": self.sem["pe"], "vector": self.sem["ve"],
                   "scalar": self.sem["ac"], "gpsimd": self.sem["gp"]}[e]
            sem.n += 1
            h = sem.h
            self.prog[e].append(lambda eng=eng, make=make, h=h: make(eng).then_inc(h, 1))
            return (sem, sem.n)
        self.prog[e].append(lambda eng=eng, make=make: make(eng))
        return None

    def mm(self, out, lhsT, rhs, start, stop, inc=False):
        return self._emit("tensor",
                          lambda eng, out=out, lhsT=lhsT, rhs=rhs, start=start, stop=stop:
                          eng.matmul(out, lhsT, rhs, start=start, stop=stop), inc)

    def tr(self, out, in_, inc=False):
        ident = self.cst_id
        return self._emit("tensor",
                          lambda eng, out=out, in_=in_, ident=ident: eng.transpose(out, in_, ident[:]), inc)

    def act(self, out, in_, func, bias=0.0, scale=1.0, accum_out=None, inc=True):
        return self._emit("scalar",
                          lambda eng, out=out, in_=in_, func=func, bias=bias, scale=scale, acc=accum_out:
                          eng.activation(out, in_, func, bias=bias, scale=scale, accum_out=acc), inc)

    def ts(self, e, out, in0, s1, s2, op0, op1, accum_out=None, inc=True):
        return self._emit(e,
                          lambda eng, out=out, in0=in0, s1=s1, s2=s2, op0=op0, op1=op1, acc=accum_out:
                          eng.tensor_scalar(out, in0, s1, s2, op0, op1, accum_out=acc), inc)

    def tt(self, e, out, in0, in1, op, inc=True):
        return self._emit(e,
                          lambda eng, out=out, in0=in0, in1=in1, op=op: eng.tensor_tensor(out, in0, in1, op), inc)

    def red(self, out, in_, op, negate=False, inc=True):
        return self._emit("vector",
                          lambda eng, out=out, in_=in_, op=op, neg=negate:
                          eng.tensor_reduce(out, in_, AX.X, op, negate=neg), inc)

    def recip(self, out, in_, inc=True):
        return self._emit("vector", lambda eng, out=out, in_=in_: eng.reciprocal(out, in_), inc)

    def copy(self, e, out, in_, inc=True):
        return self._emit(e, lambda eng, out=out, in_=in_: eng.tensor_copy(out, in_), inc)

    def bnstats(self, out, in_, inc=False):
        return self._emit("vector", lambda eng, out=out, in_=in_: eng.bn_stats(out, in_), inc)

    def bnaggr(self, out, in_, inc=True):
        return self._emit("vector", lambda eng, out=out, in_=in_: eng.bn_aggr(out, in_), inc)

    def barrier(self):
        ph = self.sem["ph"]
        for e in ENGINES:
            if e in self.qfams:
                for fam in sorted(self.qfams[e]):
                    self.w(e, (self.dsems[fam], self.dsems[fam].n))
            ph.n += 1
            eng = getattr(self.nc, e)
            h = ph.h
            self.prog[e].append(lambda eng=eng, h=h: eng.sem_inc(h, 1))
        for e in ENGINES:
            self.w(e, (ph, ph.n))

    def declare(self):
        nc, cfg = self.nc, self.cfg
        S, D, I, NH, NKV, G = (cfg[k] for k in ("S", "D", "I", "NH", "NKV", "G"))
        C, NQT, ICN, DBH, WOB = (cfg[k] for k in ("C", "NQT", "ICN", "DBH", "WOB"))
        scratch_kind = "ExternalOutput" if cfg["dbg"] else "Internal"

        self.x = nc.dram_tensor("x", [S, D], F32, kind="ExternalInput").ap()
        self.qkvblob = nc.dram_tensor("qkvblob", [NH + 2 * NKV, 128, C, 128], F32, kind="ExternalInput").ap()
        self.woblob = nc.dram_tensor("woblob", [WOB, 128, C, 512], F32, kind="ExternalInput").ap()
        self.gwblob = nc.dram_tensor("gwblob", [ICN, 128, C, 128], I32, kind="ExternalInput").ap()
        self.uwblob = nc.dram_tensor("uwblob", [ICN, 128, C, 128], I32, kind="ExternalInput").ap()
        self.dwblob = nc.dram_tensor("dwblob", [DBH, 128, ICN, 256], I32, kind="ExternalInput").ap()
        self.scal3 = nc.dram_tensor("scal3", [4], F32, kind="ExternalInput").ap()
        self.cst = nc.dram_tensor("cst", [2, 128, 128], F32, kind="ExternalInput").ap()
        self.out = nc.dram_tensor("out", [S, D], F32, kind="ExternalOutput").ap()

        self.qT_scr = nc.dram_tensor("qT_scr", [NH, 128, S], BF16, kind=scratch_kind).ap()
        self.kT_scr = nc.dram_tensor("kT_scr", [NKV, 128, S], BF16, kind=scratch_kind).ap()
        self.v_scr = nc.dram_tensor("v_scr", [NQT, 128, NKV * 128], BF16, kind=scratch_kind).ap()
        self.xnT_scr = nc.dram_tensor("xnT_scr", [C, 128, S], BF16, kind=scratch_kind).ap()
        self.r_scr = nc.dram_tensor("r_scr", [S, D], F32, kind=scratch_kind).ap()
        if cfg["dbg"]:
            self.dbg1 = nc.dram_tensor("dbg1", [128, 64], F32, kind="ExternalOutput").ap()
            self.dbg2 = nc.dram_tensor("dbg2", [128, D], F32, kind="ExternalOutput").ap()

        self.psA = [nc.alloc_psum_tensor(f"psA{i}", [128, 512], F32) for i in range(2)]
        self.psB = [nc.alloc_psum_tensor(f"psB{i}", [128, 512], F32) for i in range(2)]
        self.psC = [nc.alloc_psum_tensor(f"psC{i}", [128, 512], F32) for i in range(1)]
        self.psT = [nc.alloc_psum_tensor(f"psT{i}", [128, 1024], BF16) for i in range(3)]

        BASE, TOP = 16512, 229344
        cur = [BASE]

        def place(name, free_elems, dtype, at=None):
            esz = 2 if dtype == BF16 else 4
            off = cur[0] if at is None else at
            t = nc.alloc_sbuf_tensor_at(name, [128, free_elems], dtype, offset=off)
            if at is None:
                cur[0] = off + ((free_elems * esz + 31) // 32) * 32
            assert (off + free_elems * esz) <= TOP, f"SBUF overflow {name}"
            return t

        self.cst_id = place("cst_id", 128, BF16)
        self.cst_lt = place("cst_lt", 128, BF16)
        self.scals = place("scals", 16, F32)
        glob_end = cur[0]

        cur[0] = glob_end
        self.xnT = place("xnT", C * S, BF16)
        self.wbuf = [place(f"wbuf{i}", C * 128, BF16) for i in range(2)]
        self.xtile = [place(f"xtile{i}", D, F32) for i in range(2)]
        self.xn_tm = [place(f"xn_tm{i}", D, BF16) for i in range(2)]
        self.stg = [place(f"stg{i}", 512, BF16) for i in range(4)]
        self.vstg = [place(f"vstg{i}", 512, BF16) for i in range(2)]
        self.vtstg = [place(f"vtstg{i}", 512, BF16) for i in range(2)]
        self.bns = [place(f"bns{i}", 6 * (D // 512), F32) for i in range(2)]
        self.lnv = [place(f"lnv{i}", 8, F32) for i in range(2)]
        assert cur[0] <= TOP, f"A1 overflow {cur[0]}"

        cur[0] = glob_end
        self.Qg = place("Qg", NH * G, BF16)
        self.Kc = place("Kc", NKV * S, BF16)
        self.Vc = place("Vc", NQT * NKV * 128, BF16)
        self.Og = place("Og", NH * G, BF16)
        self.avec = [place(f"avec{i}", 16, F32) for i in range(3)]
        self.xA = [place(f"xA{i}", 512, F32) for i in range(2)]
        self.rA = [place(f"rA{i}", 512, F32) for i in range(2)]
        wo_ov = cur[0]
        self.Ss = [place(f"Ss{i}", S, F32) for i in range(3)]
        self.Pn = [place(f"Pn{i}", S, BF16) for i in range(5)]
        self.PT = place("PT", NQT * G, BF16)
        a2_end = cur[0]
        cur[0] = wo_ov
        self.wo_buf = [place(f"wo_buf{i}", C * 512, BF16) for i in range(2)]
        assert max(a2_end, cur[0]) <= TOP, f"A2 overflow {max(a2_end, cur[0])}"

        cur[0] = glob_end
        self.Hbuf = place("Hbuf", ICN * G, BF16)
        self.gtmp = [place(f"gtmp{i}", G, BF16) for i in range(2)]
        self.utmp = [place(f"utmp{i}", G, BF16) for i in range(2)]
        self.Ftmp = [place(f"Ftmp{i}", 256, F32) for i in range(2)]
        self.rB = [place(f"rB{i}", 256, F32) for i in range(2)]
        self.oB = [place(f"oB{i}", 256, F32) for i in range(2)]
        dw_ov = cur[0]
        self.xng = place("xng", C * G, BF16)
        self.gw_buf = [place(f"gw_buf{i}", C * 128, BF16) for i in range(2)]
        self.uw_buf = [place(f"uw_buf{i}", C * 128, BF16) for i in range(2)]
        b_end = cur[0]
        cur[0] = dw_ov
        self.dw_buf = [place(f"dw_buf{i}", ICN * 256, BF16) for i in range(2)]
        assert max(b_end, cur[0]) <= TOP, f"B overflow {max(b_end, cur[0])}"

    def emit_setup(self):
        k = self
        k.dma("gpsimd", "setup", k.cst_id[:], k.cst[0])
        k.dma("gpsimd", "setup", k.cst_lt[:], k.cst[1])
        for i in range(4):
            last = k.dma("gpsimd", "setup", k.scals[:, i:i + 1], k.scal3[None, i:i + 1].broadcast_to([128, 1]))
        self.setup_ld = last

    def emit_a1(self):
        k = self
        cfg = self.cfg
        S, D, NH, NKV, C, NQT = (cfg[x] for x in ("S", "D", "NH", "NKV", "C", "NQT"))
        xnT_v = k.xnT.ap().rearrange("p (c t) -> p c t", c=C)

        xt_rel = [None, None]
        xn_rel = [None, None]
        trp_rel = [None, None]
        lnv_rel = [None, None]
        nb = D // 512
        tr_ctr = 0
        for t in range(NQT):
            sl = t % 2
            k.w("sync", xt_rel[sl])
            x_ld = k.dma("sync", f"xt{sl}", k.xtile[sl][:], k.x[t * 128:(t + 1) * 128, :])
            k.w("vector", x_ld)
            v_bns = None
            for b in range(nb):
                v_bns = k.bnstats(k.bns[sl][:, 6 * b:6 * (b + 1)], k.xtile[sl][:, 512 * b:512 * (b + 1)],
                                  inc=(b == nb - 1))
            k.w("vector", v_bns)
            k.w("vector", lnv_rel[sl])
            k.bnaggr(k.lnv[sl][:, 0:2], k.bns[sl][:])
            v_aggr = (k.sem["ve"], k.sem["ve"].n)
            k.w("scalar", v_aggr)
            a_std = k.act(k.lnv[sl][:, 2:3], k.lnv[sl][:, 1:2], AF.Sqrt, bias=k.scals[:, 3:4], scale=1.0)
            lnv_rel[sl] = a_std
            k.w("vector", a_std)
            v_rc = k.recip(k.lnv[sl][:, 3:4], k.lnv[sl][:, 2:3])
            k.w("vector", v_rc)
            v_nm = k.ts("vector", k.lnv[sl][:, 4:5], k.lnv[sl][:, 0:1], k.lnv[sl][:, 3:4], -1.0,
                        ALU.mult, ALU.mult)
            k.w("vector", v_nm)
            k.w("vector", xn_rel[sl])
            v_norm = k.ts("vector", k.xn_tm[sl][:], k.xtile[sl][:], k.lnv[sl][:, 3:4], k.lnv[sl][:, 4:5],
                          ALU.mult, ALU.add)
            xt_rel[sl] = v_norm
            k.w("tensor", v_norm)
            for cp in range(0, C, 4):
                npk = min(4, C - cp)
                tsl = tr_ctr % 2
                tr_ctr += 1
                k.w("tensor", trp_rel[tsl])
                pe_c = None
                for j in range(npk):
                    pe_c = k.tr(k.psT[tsl][:, 128 * j:128 * (j + 1)],
                                k.xn_tm[sl][:, 128 * (cp + j):128 * (cp + j + 1)],
                                inc=(j == npk - 1))
                xn_rel[sl] = pe_c
                k.w("vector", pe_c)
                trp_rel[tsl] = k.copy("vector", xnT_v[:, cp:cp + npk, t * 128:(t + 1) * 128],
                                      k.psT[tsl][:].rearrange("p (c t) -> p c t", c=8)[:, 0:npk, :])
            if t == 0 and self.cfg["dbg"]:
                k.w("sync", (k.sem["ve"], k.sem["ve"].n))
                k.dma("sync", "dbgf", k.dbg1[:, 0:16], k.scals[:])
                k.dma("sync", "dbgf", k.dbg1[:, 16:24], k.lnv[0][:])
                k.dma("sync", "dbgf", k.dbg1[:, 24:24 + 6 * (D // 512)], k.bns[0][:])
                k.w("gpsimd", (k.sem["ve"], k.sem["ve"].n))
                k.dma("gpsimd", "dbgf2", k.dbg2[:, 0:D], k.xn_tm[0][:])
        ln_done = (k.sem["ve"], k.sem["ve"].n)
        k.w("sync", ln_done)
        for c in range(C):
            k.dma("sync", "xnw", k.xnT_scr[c], xnT_v[:, c, :])

        k.w("tensor", ln_done)
        NOC = NH + 2 * NKV
        w_rel = [None, None]
        ps_rel = [None, None]
        st_rel = [None] * 4
        vs_rel = [None, None]
        vt_rel = [None] * 2
        ntc = S // 512
        step = 0
        for oc in range(NOC):
            wsl = oc % 2
            k.w("gpsimd", w_rel[wsl])
            w_ld = k.dma("gpsimd", f"wb{wsl}", k.wbuf[wsl][:], k.qkvblob[oc].rearrange("p c i -> p (c i)"),
                         max_dma_last_dim=8192)
            k.w("tensor", w_ld)
            is_v = oc >= NH + NKV
            for tc in range(ntc):
                psl = step % 2
                step += 1
                k.w("tensor", ps_rel[psl])
                pe_c = None
                for c in range(C):
                    pe_c = k.mm(k.psB[psl][:], k.wbuf[wsl][:, c * 128:(c + 1) * 128],
                                xnT_v[:, c, tc * 512:(tc + 1) * 512],
                                start=(c == 0), stop=(c == C - 1), inc=(c == C - 1))
                w_rel[wsl] = pe_c
                k.w("scalar", pe_c)
                if not is_v:
                    ssl = step % 4
                    k.w("scalar", st_rel[ssl])
                    a_c = k.act(k.stg[ssl][:], k.psB[psl][:], AF.Copy)
                    ps_rel[psl] = a_c
                    k.w("sync", a_c)
                    if oc < NH:
                        dst = k.qT_scr[oc][:, tc * 512:(tc + 1) * 512]
                    else:
                        dst = k.kT_scr[oc - NH][:, tc * 512:(tc + 1) * 512]
                    st_rel[ssl] = k.dma("sync", f"stg{ssl}", dst, k.stg[ssl][:])
                else:
                    vsl = step % 2
                    k.w("scalar", vs_rel[vsl])
                    a_c = k.act(k.vstg[vsl][:], k.psB[psl][:], AF.Copy)
                    ps_rel[psl] = a_c
                    k.w("tensor", a_c)
                    tsl = step % 2
                    k.w("tensor", trp_rel[tsl])
                    pe_t = None
                    for j in range(4):
                        pe_t = k.tr(k.psT[tsl][:, 128 * j:128 * (j + 1)],
                                    k.vstg[vsl][:, 128 * j:128 * (j + 1)], inc=(j == 3))
                    vs_rel[vsl] = pe_t
                    k.w("scalar", pe_t)
                    k.w("scalar", vt_rel[tsl])
                    a_t = k.act(k.vtstg[tsl][:], k.psT[tsl][:, 0:512], AF.Copy)
                    trp_rel[tsl] = a_t
                    k.w("sync", a_t)
                    kv = oc - NH - NKV
                    last = None
                    for j in range(4):
                        tt_ = tc * 4 + j
                        last = k.dma("sync", f"vt{tsl}", k.v_scr[tt_][:, kv * 128:(kv + 1) * 128],
                                     k.vtstg[tsl][:, 128 * j:128 * (j + 1)])
                    vt_rel[tsl] = last

    def emit_a2(self):
        k = self
        cfg = self.cfg
        S, NH, NKV, G = (cfg[x] for x in ("S", "NH", "NKV", "G"))
        C, NQT, QTG, WOB, NREP = (cfg[x] for x in ("C", "NQT", "QTG", "WOB", "NREP"))
        SCALE = cfg["SCALE"]
        PT_v = k.PT.ap().rearrange("p (j q) -> p j q", j=NQT)
        NSL = 3
        NPN = 5
        LAG_T = 3
        LAG_PV = 5

        ckps = [k.psA[0], k.psA[1], k.psC[0]]
        ssp_rel = [None] * 3
        Ss_rel = [None] * NSL
        Pn_rel = [None] * NPN
        av_rel = [None] * NSL
        ptp_rel = [None, None, None]
        ot_rel = [None, None]
        og_rel = None
        qg_rel = None
        wo_rel = [None, None]
        woA_rel = [None, None]
        xA_rel = [None, None]
        rA_rel = [None, None]
        wo_done = None
        ck_ctr = 0
        step = 0
        pt_ctr = 0

        for g in range(cfg["NG"]):
            k.w("scalar", qg_rel)
            q_ld = None
            kv_ld = None
            for h in range(NH):
                q_ld = k.dma("scalar", "qg", k.Qg[:, h * G:(h + 1) * G], k.qT_scr[h][:, g * G:(g + 1) * G])
            for kv in range(NKV):
                kv_ld = k.dma("scalar", "kv", k.Kc[:, kv * S + g * G: kv * S + (g + 1) * G],
                              k.kT_scr[kv][:, g * G:(g + 1) * G])
            for j in range(QTG):
                tt_ = g * QTG + j
                kv_ld = k.dma("scalar", "kv", k.Vc[:, tt_ * NKV * 128:(tt_ + 1) * NKV * 128], k.v_scr[tt_])
            k.w("tensor", q_ld)
            k.w("tensor", kv_ld)

            deferred = []

            def emit_chain(h, qt, sl):
                i = g * QTG + qt
                L = (i + 1) * 128
                nck = (L + 511) // 512
                k.w("vector", (k.sem["ve"], k.sem["ve"].n))
                k.red(k.avec[sl][:, 8:9], k.avec[sl][:, 0:nck], ALU.max, negate=True)
                v_nmax = (k.sem["ve"], k.sem["ve"].n)
                k.w("scalar", v_nmax)
                psl = step_of[(h, qt)] % NPN
                k.w("scalar", Pn_rel[psl])
                k.w("scalar", wo_done)
                a_exp = k.act(k.Pn[psl][:, 0:L], k.Ss[sl][:, 0:L], AF.Exp,
                              bias=k.avec[sl][:, 8:9], scale=1.0,
                              accum_out=k.avec[sl][:, 9:10])
                Ss_rel[sl] = a_exp
                av_rel[sl] = a_exp
                k.w("vector", a_exp)
                v_rc = k.recip(k.avec[sl][:, 10:11], k.avec[sl][:, 9:10])
                k.w("gpsimd", v_rc)
                v_norm = k.ts("gpsimd", k.Pn[psl][:, 0:L], k.Pn[psl][:, 0:L],
                              k.avec[sl][:, 10:11], 1.0, ALU.mult, ALU.mult)
                return v_norm

            def make_transposes(h, qt, v_norm):
                def go():
                    nonlocal pt_ctr
                    i = g * QTG + qt
                    psl = step_of[(h, qt)] % NPN
                    k.w("tensor", v_norm)
                    pe_tr = None
                    for jp in range(0, i + 1, 8):
                        npk = min(8, i + 1 - jp)
                        tsl = pt_ctr % 3
                        pt_ctr += 1
                        k.w("tensor", ptp_rel[tsl])
                        for j in range(jp, jp + npk):
                            pe_tr = k.tr(k.psT[tsl][:, 128 * (j - jp):128 * (j - jp + 1)],
                                         k.Pn[psl][:, j * 128:(j + 1) * 128], inc=(j == jp + npk - 1))
                        k.w("scalar", pe_tr)
                        a_pt = k.act(PT_v[:, jp:jp + npk, qt * 128:(qt + 1) * 128],
                                     k.psT[tsl][:].rearrange("p (j q) -> p j q", j=8)[:, 0:npk, :],
                                     AF.Copy)
                        ptp_rel[tsl] = a_pt
                    Pn_rel[psl] = pe_tr
                return go

            def make_pv(h):
                def go():
                    nonlocal og_rel
                    kv = h // NREP
                    hsl = h % 2
                    k.w("tensor", ptp_rel[0])
                    k.w("tensor", ptp_rel[1])
                    k.w("tensor", ptp_rel[2])
                    k.w("tensor", ot_rel[hsl])
                    jmax = g * QTG + QTG - 1
                    k.mm(k.psB[hsl][:, 0:G], k.cst_id[:], k.Kc[:, kv * S: kv * S + G],
                         start=True, stop=True)
                    pe_pv = None
                    for j in range(jmax + 1):
                        jj = j - g * QTG
                        q0 = 0 if jj < 0 else jj * 128
                        pe_pv = k.mm(k.psB[hsl][:, q0:G],
                                     k.Vc[:, (j * NKV + kv) * 128:(j * NKV + kv + 1) * 128],
                                     PT_v[:, j, q0:G], start=(j == 0), stop=(j == jmax), inc=(j == jmax))
                    k.w("scalar", pe_pv)
                    k.w("scalar", og_rel)
                    ot_rel[hsl] = k.act(k.Og[:, h * G:(h + 1) * G], k.psB[hsl][:, 0:G], AF.Copy)
                return go

            steps = [(h, qt) for h in range(NH) for qt in range(QTG)]
            step_of = {s: idx for idx, s in enumerate(steps)}
            for idx, (h, qt) in enumerate(steps):
                sl = idx % NSL
                i = g * QTG + qt
                L = (i + 1) * 128
                nck = (L + 511) // 512
                kv = h // NREP
                step += 1
                for kc in range(nck):
                    Lc = min(512, L - kc * 512)
                    csl = ck_ctr % 3
                    ck_ctr += 1
                    k.w("tensor", ssp_rel[csl])
                    k.w("tensor", wo_done)
                    if kc == 0:
                        for _ in range(2):
                            k.mm(ckps[csl][:, 0:512], k.cst_id[:], k.Kc[:, kv * S: kv * S + 512],
                                 start=True, stop=True)
                    diag = (kc == nck - 1)
                    pe_c = k.mm(ckps[csl][:, 0:Lc], k.Qg[:, h * G + qt * 128: h * G + (qt + 1) * 128],
                                k.Kc[:, kv * S + kc * 512: kv * S + kc * 512 + Lc],
                                start=True, stop=not diag, inc=not diag)
                    if diag:
                        pe_c = k.mm(ckps[csl][:, Lc - 128:Lc], k.cst_lt[:], k.cst_id[:],
                                    start=False, stop=True, inc=True)
                    k.w("vector", pe_c)
                    if kc == 0:
                        k.w("vector", Ss_rel[sl])
                        k.w("vector", av_rel[sl])
                        k.w("vector", wo_done)
                    ve_c = k.ts("vector", k.Ss[sl][:, kc * 512:kc * 512 + Lc], ckps[csl][:, 0:Lc],
                                SCALE, None, ALU.mult, ALU.max,
                                accum_out=k.avec[sl][:, kc:kc + 1])
                    ssp_rel[csl] = ve_c
                qg_rel = (k.sem["pe"], k.sem["pe"].n)
                v_norm = emit_chain(h, qt, sl)
                deferred.append((idx + LAG_T, make_transposes(h, qt, v_norm)))
                if qt == QTG - 1:
                    deferred.append((idx + LAG_PV, make_pv(h)))
                while deferred and deferred[0][0] <= idx:
                    deferred.pop(0)[1]()
            while deferred:
                deferred.pop(0)[1]()

            hl_ve = (k.sem["ve"], k.sem["ve"].n)
            hl_ac = (k.sem["ac"], k.sem["ac"].n)
            hl_pe = (k.sem["pe"], k.sem["pe"].n)
            ot_done = hl_ac
            for db in range(WOB):
                wsl = db % 2
                k.w("gpsimd", wo_rel[wsl])
                if db < 2:
                    k.w("gpsimd", hl_pe)
                    k.w("gpsimd", hl_ve)
                    k.w("gpsimd", hl_ac)
                wo_ld = k.dma("gpsimd", f"wo{wsl}", k.wo_buf[wsl][:], k.woblob[db].rearrange("p c j -> p (c j)"),
                              max_dma_last_dim=8192)
                for tc in range(QTG):
                    psl = (db * QTG + tc) % 2
                    k.w("tensor", woA_rel[psl])
                    k.w("tensor", ssp_rel[psl])
                    k.w("tensor", wo_ld)
                    k.w("tensor", ot_done)
                    pe_c = None
                    for c in range(C):
                        pe_c = k.mm(k.psA[psl][:], k.Og[:, c * G + tc * 128: c * G + (tc + 1) * 128],
                                    k.wo_buf[wsl][:, c * 512:(c + 1) * 512],
                                    start=(c == 0), stop=(c == C - 1), inc=(c == C - 1))
                    wo_rel[wsl] = pe_c
                    og_rel = pe_c
                    xsl = (db * QTG + tc) % 2
                    k.w("sync", xA_rel[xsl])
                    t0 = (g * QTG + tc) * 128
                    x_ld = k.dma("sync", f"xa{xsl}", k.xA[xsl][:], k.x[t0:t0 + 128, db * 512:(db + 1) * 512])
                    k.w("vector", x_ld)
                    k.w("vector", pe_c)
                    k.w("vector", rA_rel[xsl])
                    ve_r = k.tt("vector", k.rA[xsl][:], k.psA[psl][:], k.xA[xsl][:], ALU.add)
                    woA_rel[psl] = ve_r
                    ssp_rel[psl] = ve_r
                    xA_rel[xsl] = ve_r
                    k.w("sync", ve_r)
                    rA_rel[xsl] = k.dma("sync", f"ra{xsl}", k.r_scr[t0:t0 + 128, db * 512:(db + 1) * 512], k.rA[xsl][:])
            wo_done = og_rel

    def emit_b(self):
        k = self
        cfg = self.cfg
        S, D, I, G = (cfg[x] for x in ("S", "D", "I", "G"))
        C, ICN, DBH, QTG = (cfg[x] for x in ("C", "ICN", "DBH", "QTG"))
        Hv = k.Hbuf.ap().rearrange("p (ic t) -> p ic t", ic=ICN)
        xng_v = k.xng.ap().rearrange("p (c t) -> p c t", c=C)

        gw_rel = [None, None]
        uw_rel = [None, None]
        dw_rel = [None, None]
        gps_rel = [None, None]
        ups_rel = [None, None]
        fps_rel = [None, None]
        gt_rel = [None, None]
        ut_rel = [None, None]
        Ft_rel = [None, None]
        rB_rel = [None, None]
        oB_rel = [None, None]
        prev_down = None

        for g in range(cfg["NG"]):
            k.w("sync", prev_down)
            xng_ld = None
            for c in range(C):
                xng_ld = k.dma("sync", "xng", xng_v[:, c, :], k.xnT_scr[c][:, g * G:(g + 1) * G])
            for ic in range(ICN):
                wsl = ic % 2
                k.w("gpsimd", gw_rel[wsl])
                if ic < 2:
                    k.w("gpsimd", prev_down)
                g_ld = k.dma("gpsimd", f"gw{wsl}", k.gw_buf[wsl][:], k.gwblob[ic].rearrange("p c i -> p (c i)"),
                             max_dma_last_dim=8192)
                k.w("gpsimd", uw_rel[wsl])
                u_ld = k.dma("gpsimd", f"uw{wsl}", k.uw_buf[wsl][:], k.uwblob[ic].rearrange("p c i -> p (c i)"),
                             max_dma_last_dim=8192)
                k.w("tensor", g_ld)
                k.w("tensor", u_ld)
                k.w("tensor", xng_ld)
                psl = ic % 2
                k.w("tensor", gps_rel[psl])
                pe_g = None
                for c in range(C):
                    pe_g = k.mm(k.psB[psl][:, 0:G], k.gw_buf[wsl][:, c * 128:(c + 1) * 128],
                                xng_v[:, c, :], start=(c == 0), stop=(c == C - 1), inc=(c == C - 1))
                gw_rel[wsl] = pe_g
                upb = [k.psA[1], k.psC[0]][psl]
                k.w("tensor", ups_rel[psl])
                k.w("tensor", fps_rel[1] if psl == 0 else None)
                pe_u = None
                for c in range(C):
                    pe_u = k.mm(upb[:, 0:G], k.uw_buf[wsl][:, c * 128:(c + 1) * 128],
                                xng_v[:, c, :], start=(c == 0), stop=(c == C - 1), inc=(c == C - 1))
                uw_rel[wsl] = pe_u
                k.w("scalar", pe_g)
                k.w("scalar", gt_rel[psl])
                a_g = k.act(k.gtmp[psl][:], k.psB[psl][:, 0:G], AF.Silu, bias=0.0, scale=k.scals[:, 0:1])
                gps_rel[psl] = a_g
                k.w("scalar", pe_u)
                k.w("scalar", ut_rel[psl])
                a_u = k.act(k.utmp[psl][:], upb[:, 0:G], AF.Copy, bias=0.0, scale=k.scals[:, 1:2])
                ups_rel[psl] = a_u
                k.w("vector", a_g)
                k.w("vector", a_u)
                if ic < 2:
                    k.w("vector", prev_down)
                ve_h = k.tt("vector", Hv[:, ic, :], k.gtmp[psl][:], k.utmp[psl][:], ALU.mult)
                gt_rel[psl] = ve_h
                ut_rel[psl] = ve_h
            h_done = (k.sem["ve"], k.sem["ve"].n)
            gu_done = (k.sem["pe"], k.sem["pe"].n)
            for dbh in range(DBH):
                wsl = dbh % 2
                k.w("gpsimd", dw_rel[wsl])
                if dbh < 2:
                    k.w("gpsimd", gu_done)
                d_ld = k.dma("gpsimd", f"dw{wsl}", k.dw_buf[wsl][:], k.dwblob[dbh].rearrange("p ic o -> p (ic o)"),
                             max_dma_last_dim=8192)
                for tc in range(QTG):
                    psl = (dbh * QTG + tc) % 2
                    k.w("tensor", fps_rel[psl])
                    if psl == 1:
                        k.w("tensor", ups_rel[0])
                    k.w("tensor", d_ld)
                    k.w("tensor", h_done)
                    pe_f = None
                    for ic in range(ICN):
                        pe_f = k.mm(k.psA[psl][:, 0:256], Hv[:, ic, tc * 128:(tc + 1) * 128],
                                    k.dw_buf[wsl][:, ic * 256:(ic + 1) * 256],
                                    start=(ic == 0), stop=(ic == ICN - 1), inc=(ic == ICN - 1))
                    dw_rel[wsl] = pe_f
                    prev_down = pe_f
                    k.w("scalar", pe_f)
                    esl = (dbh * QTG + tc) % 2
                    k.w("scalar", Ft_rel[esl])
                    a_f = k.act(k.Ftmp[esl][:], k.psA[psl][:, 0:256], AF.Copy, bias=0.0, scale=k.scals[:, 2:3])
                    fps_rel[psl] = a_f
                    t0 = (g * QTG + tc) * 128
                    o0 = dbh * 256
                    k.w("sync", rB_rel[esl])
                    r_ld = k.dma("sync", f"rb{esl}", k.rB[esl][:], k.r_scr[t0:t0 + 128, o0:o0 + 256])
                    k.w("vector", r_ld)
                    k.w("vector", a_f)
                    k.w("vector", oB_rel[esl])
                    ve_o = k.tt("vector", k.oB[esl][:], k.Ftmp[esl][:], k.rB[esl][:], ALU.add)
                    Ft_rel[esl] = ve_o
                    rB_rel[esl] = ve_o
                    k.w("sync", ve_o)
                    oB_rel[esl] = k.dma("sync", f"ob{esl}", k.out[t0:t0 + 128, o0:o0 + 256], k.oB[esl][:])

    def finish(self):
        for q in ("sync", "scalar", "gpsimd"):
            for fam in sorted(self.qfams[q]):
                self.w(q, (self.dsems[fam], self.dsems[fam].n))

    def build(self):
        self.declare()
        self.emit_setup()
        for e in ("scalar", "vector", "tensor"):
            self.w(e, self.setup_ld)
        ph = self.cfg["phases"]
        if "A1" in ph:
            self.emit_a1()
        if "A2" in ph:
            self.barrier()
            self.emit_a2()
        if "B" in ph:
            self.barrier()
            self.emit_b()
        self.finish()
        nc = self.nc
        with nc.Block() as block:
            for e in ENGINES:
                getattr(block, e)(lambda eng, e=e: [f() for f in self.prog[e]])
        return nc


def host_blobs(cfg, wq, wk, wv, wo, gw, uw, dw):
    C, NH, NKV, ICN, DBH, WOB = (cfg[x] for x in ("C", "NH", "NKV", "ICN", "DBH", "WOB"))
    qkv = np.concatenate([np.asarray(wq), np.asarray(wk), np.asarray(wv)], axis=1)
    qkvblob = np.ascontiguousarray(qkv.reshape(C, 128, NH + 2 * NKV, 128).transpose(2, 1, 0, 3))
    woblob = np.ascontiguousarray(np.asarray(wo).reshape(C, 128, WOB, 512).transpose(2, 1, 0, 3))
    gwblob = np.ascontiguousarray(np.asarray(gw).reshape(ICN, 128, C, 128).transpose(0, 3, 2, 1))
    uwblob = np.ascontiguousarray(np.asarray(uw).reshape(ICN, 128, C, 128).transpose(0, 3, 2, 1))
    dwblob = np.ascontiguousarray(np.asarray(dw).reshape(DBH, 256, ICN, 128).transpose(0, 3, 2, 1))
    ident = np.eye(128, dtype=np.float32)
    lt = np.where(np.arange(128)[:, None] > np.arange(128)[None, :], np.float32(NEG_BIG), np.float32(0.0))
    cst = np.stack([ident, lt]).astype(np.float32)
    return dict(qkvblob=qkvblob, woblob=woblob, gwblob=gwblob, uwblob=uwblob, dwblob=dwblob, cst=cst)


def run_cores(cfg, inputs, n_cores, **kw):
    x = np.asarray(inputs["L0_attn_x"])
    blobs = host_blobs(cfg, inputs["L0_attn_wq"], inputs["L0_attn_wk"], inputs["L0_attn_wv"],
                       inputs["L0_attn_wo"], inputs["L0_ffn_gate_w"], inputs["L0_ffn_up_w"],
                       inputs["L0_ffn_down_w"])
    scal3 = np.stack([np.asarray(inputs["L0_ffn_gate_s"]).reshape(()),
                      np.asarray(inputs["L0_ffn_up_s"]).reshape(()),
                      np.asarray(inputs["L0_ffn_down_s"]).reshape(()),
                      np.float32(1e-5)]).astype(np.float32)
    kb = KB(cfg)
    nc = kb.build()
    in_maps = [dict(x=np.ascontiguousarray(x[b]), scal3=scal3, **blobs) for b in range(n_cores)]
    return run_bass_kernel_spmd(nc, in_maps, core_ids=list(range(n_cores)), **kw)


def kernel(**inputs):
    cfg = make_cfg()
    res = run_cores(cfg, inputs, 8)
    return np.stack([r["out"] for r in res.results], axis=0)
